# version 10
# baseline (speedup 1.0000x reference)
import numpy as np
import ml_dtypes
from contextlib import ExitStack

import concourse.bass as bass
import concourse.mybir as mybir
from concourse.bass_utils import run_bass_kernel_spmd

F32 = mybir.dt.float32
BF16 = mybir.dt.bfloat16
ALU = mybir.AluOpType
ACTF = mybir.ActivationFunctionType
AXX = mybir.AxisListType.X
BF = ml_dtypes.bfloat16

NEG = -1.0e9
D = 1024
NH = 16
HDIM = 64
WIN = 128
LATD = 256
LHN = 4
NLAT = 64
EE = 8
HFF = 1024
NCORES = 8


class Sem:
    def __init__(self, h):
        self.h = h
        self.n = 0

    def bump(self, k=1):
        self.n += k
        return self.n


class Plan:
    ENGS = ("sync", "gpsimd", "vector", "scalar", "tensor")

    def __init__(self, nc):
        self.nc = nc
        self.q = {k: [] for k in self.ENGS}

    def E(self, eng, meth, *args, waits=(), inc=None, **kw):
        waits = tuple(w for w in waits if w is not None)

        def f(e, meth=meth, args=args, kw=kw, waits=waits, inc=inc):
            for (s, v) in waits:
                e.wait_ge(s.h, v)
            r = getattr(e, meth)(*args, **kw)
            if inc is not None:
                r.then_inc(inc[0].h, inc[1])

        self.q[eng].append(f)

    def W(self, eng, sem, val):
        def f(e, sem=sem, val=val):
            e.wait_ge(sem.h, val)

        self.q[eng].append(f)

    def D(self, eng):
        def f(e):
            e.drain()

        self.q[eng].append(f)

    def flush(self):
        qs = self.q
        self.q = {k: [] for k in self.ENGS}
        with self.nc.Block() as blk:
            @blk.sync
            def _(e):
                for f in qs["sync"]:
                    f(e)

            @blk.gpsimd
            def _(e):
                for f in qs["gpsimd"]:
                    f(e)

            @blk.vector
            def _(e):
                for f in qs["vector"]:
                    f(e)

            @blk.scalar
            def _(e):
                for f in qs["scalar"]:
                    f(e)

            @blk.tensor
            def _(e):
                for f in qs["tensor"]:
                    f(e)



def host_constants(S):
    PC = S // NLAT
    LPT = 128 // PC
    c = {}
    c["ident"] = np.eye(128, dtype=BF)
    q = np.arange(128)[:, None]
    k = np.arange(256)[None, :]
    band = (k >= q) & (k <= q + 127)
    c["mloc"] = np.stack([band.astype(np.float32),
                          (band & (k >= 128)).astype(np.float32)]).astype(BF)
    r = np.arange(128)[:, None]
    u = np.arange(128)[None, :]
    c["mlat"] = ((u - 64) < (r // PC)).astype(np.float32).astype(BF)
    c["indp"] = (
        (r // PC == np.arange(LPT)[None, :]).astype(np.float32) / PC
    ).astype(BF)
    es = np.zeros((8, 16, 128), dtype=BF)
    for cc in range(8):
        for p in range(128):
            es[cc, 2 * cc + p // 64, p] = 1.0
    c["estk"] = np.ascontiguousarray(es.transpose(1, 0, 2).reshape(16, 1024))
    return c


def host_prep(inputs, core, S, NSEQ):
    A = np.ascontiguousarray
    ins = {}
    x = np.asarray(inputs["L0_attn_x"])
    ins["x"] = A(x[core * NSEQ:(core + 1) * NSEQ].reshape(NSEQ * S, D))
    ins["wqkv"] = A(np.asarray(inputs["L0_attn_qkv_w"]))
    ins["wrq"] = A(np.asarray(inputs["L0_attn_rq_w"]))
    ins["wrkv"] = A(np.asarray(inputs["L0_attn_rkv_w"]))
    ins["wrout"] = A(np.asarray(inputs["L0_attn_rout_w"]))
    ins["wout"] = A(np.asarray(inputs["L0_attn_out_w"]))
    qkvb = np.asarray(inputs["L0_attn_qkv_b"])
    ins["qkvb"] = A(qkvb[:2048].reshape(16, 128).T)
    ins["vbrow"] = A(qkvb[2048:].reshape(1, D))
    ins["rqb"] = A(np.asarray(inputs["L0_attn_rq_b"]).reshape(2, 128).T)
    ins["rkvb"] = A(np.asarray(inputs["L0_attn_rkv_b"]).reshape(4, 128).T)
    ins["routb"] = A(np.asarray(inputs["L0_attn_rout_b"]).reshape(8, 128).T)
    ins["outbrow"] = A(np.asarray(inputs["L0_attn_out_b"]).reshape(1, D))
    gl = np.asarray(inputs["L0_attn_gl_w"])[:, 0]
    gr = np.asarray(inputs["L0_attn_gr_w"])[:, 0]
    gstk = np.zeros((2, 8, 128, 16), dtype=np.float32)
    for cc in range(8):
        for half in range(2):
            h = 2 * cc + half
            gstk[0, cc, half * 64:(half + 1) * 64, h] = gl
            gstk[1, cc, half * 64:(half + 1) * 64, h] = gr
    ins["gstk"] = A(gstk.transpose(2, 0, 1, 3).reshape(128, 256))
    wgtT = np.asarray(inputs["L0_ffn_gate_w"]).T
    ins["wgt"] = A(wgtT.reshape(8, 128, EE).transpose(1, 0, 2).reshape(128, 64))
    ins["lb"] = A(np.tile(np.asarray(inputs["L0_ffn_lb_bias"]).reshape(1, EE),
                          (128, 1)))
    ins["w1t"] = A(np.asarray(inputs["L0_ffn_w1"]).transpose(0, 2, 1))
    ins["w2t"] = A(np.asarray(inputs["L0_ffn_w2"]).transpose(0, 2, 1))
    ins["cfct"] = A(np.asarray(inputs["L0_ffn_cfc"]).T)
    ins["cprjt"] = A(np.asarray(inputs["L0_ffn_cproj"]).T)
    return ins



LEVEL = 9
SUB5 = 8


def build(S=2048, NSEQ=2, CHA=512, CHB=512):
    PC = S // NLAT
    LPT = 128 // PC
    NTA = CHA // 128
    NCH_A = S // CHA
    NTB = CHB // 128
    NCH_B = (NSEQ * S) // CHB
    NLC = CHA // PC

    nc = bass.Bass("TRN2", target_bir_lowering=False, debug=False)
    dt = nc.dram_tensor

    x_d = dt("x", [NSEQ * S, D], F32, kind="ExternalInput").ap()
    out_d = dt("out", [NSEQ * S, D], F32, kind="ExternalOutput").ap()
    x2_d = dt("x2s", [NSEQ * S, D], F32, kind="Internal").ap()

    din = {}
    for name, shp in [
        ("wqkv", [D, 3 * D]), ("wrq", [D, LATD]), ("wrkv", [D, 2 * LATD]),
        ("wrout", [LATD, D]), ("wout", [D, D]), ("qkvb", [128, 16]),
        ("vbrow", [1, D]), ("rqb", [128, 2]), ("rkvb", [128, 4]),
        ("routb", [128, 8]), ("outbrow", [1, D]),
        ("gstk", [128, 256]), ("wgt", [128, 64]), ("lb", [128, EE]),
        ("w1t", [EE, D, HFF]), ("w2t", [EE, HFF, D]),
        ("cfct", [D, HFF]), ("cprjt", [HFF, D]),
    ]:
        din[name] = dt(name, shp, F32, kind="ExternalInput").ap()
    for name, shp in [("ident", [128, 128]), ("indp", [128, LPT]),
                      ("estk", [16, 1024]), ("mloc", [2, 128, 256]),
                      ("mlat", [128, 128])]:
        din[name] = dt(name, shp, BF16, kind="ExternalInput").ap()

    es = ExitStack()
    with es:
        EC = es.enter_context
        sLX = Sem(EC(nc.semaphore("sLX")))
        sWG = Sem(EC(nc.semaphore("sWG")))
        sX = [Sem(EC(nc.semaphore(f"sX{i}"))) for i in range(2)]
        sW = [Sem(EC(nc.semaphore(f"sW{i}"))) for i in range(3)]
        sW2s = [Sem(EC(nc.semaphore(f"sW2_{i}"))) for i in range(3)]
        sWA = Sem(EC(nc.semaphore("sWA")))
        sSX = [Sem(EC(nc.semaphore(f"sSX{i}"))) for i in range(2)]
        sT = Sem(EC(nc.semaphore("sT")))
        sV = Sem(EC(nc.semaphore("sV")))
        sS = Sem(EC(nc.semaphore("sS")))
        sST = Sem(EC(nc.semaphore("sST")))
        sG = Sem(EC(nc.semaphore("sG")))

        P = Plan(nc)

        pA = EC(nc.psum_tensor("pA", [128, 512], F32))
        pB = EC(nc.psum_tensor("pB", [128, 512], F32))
        pC = EC(nc.psum_tensor("pC", [128, 1024], F32))
        pD = EC(nc.psum_tensor("pD", [128, 1024], F32))
        pT1 = EC(nc.psum_tensor("pT1", [128, 1024], BF16))
        pG = EC(nc.psum_tensor("pG", [128, 512], F32))

        ident = EC(nc.sbuf_tensor("sb_ident", [128, 128], BF16))
        mloc = EC(nc.sbuf_tensor("sb_mloc", [128, 512], BF16))
        mlat = EC(nc.sbuf_tensor("sb_mlat", [128, 128], BF16))
        indp = EC(nc.sbuf_tensor("sb_indp", [128, LPT], BF16))
        estk = EC(nc.sbuf_tensor("sb_estk", [16, 1024], BF16))
        gstk = EC(nc.sbuf_tensor("sb_gstk", [128, 256], BF16))
        qkvb = EC(nc.sbuf_tensor("sb_qkvb", [128, 16], F32))
        rqb = EC(nc.sbuf_tensor("sb_rqb", [128, 2], F32))
        rkvb = EC(nc.sbuf_tensor("sb_rkvb", [128, 4], F32))
        routb = EC(nc.sbuf_tensor("sb_routb", [128, 8], F32))
        vbrow = EC(nc.sbuf_tensor("sb_vbrow", [1, D], BF16))
        outbrow = EC(nc.sbuf_tensor("sb_outbrow", [1, D], BF16))
        onesrow = EC(nc.sbuf_tensor("onesrow", [1, 512], BF16))
        wgt = EC(nc.sbuf_tensor("sb_wgt", [128, 8 * EE], BF16))
        seps = EC(nc.sbuf_tensor("sb_eps", [128, 1], F32))
        seps30 = EC(nc.sbuf_tensor("sb_eps30", [128, 1], F32))
        sone1 = EC(nc.sbuf_tensor("sb_one1", [128, 1], F32))
        seps6 = EC(nc.sbuf_tensor("sb_eps6", [128, 1], F32))
        lb = EC(nc.sbuf_tensor("sb_lb", [128, EE], F32))

        def ld_sync(dst_ap, src_ap):
            P.E("sync", "dma_start", out=dst_ap, in_=src_ap, inc=(sLX, 16))
            return sLX.bump(16)

        def ld_cast(dst_ap, src_ap, sm=None):
            sm = sm or sWG
            P.E("gpsimd", "dma_start", out=dst_ap, in_=src_ap, inc=(sm, 16))
            return sm.bump(16)

        ld_cast_dummy = None
        ld_sync(qkvb[:], din["qkvb"])
        ld_sync(rqb[:], din["rqb"])
        ld_sync(rkvb[:], din["rkvb"])
        ld_sync(routb[:], din["routb"])
        ld_sync(lb[:], din["lb"])
        ld_cast(ident[:], din["ident"])
        ld_cast(indp[:], din["indp"])
        ld_cast(estk[:], din["estk"])
        ld_cast(mloc[:].rearrange("p (a k) -> p a k", a=2),
                din["mloc"].rearrange("a p k -> p a k"))
        ld_cast(mlat[:], din["mlat"])
        ld_cast(gstk[:], din["gstk"])
        ld_cast(vbrow[:], din["vbrow"])
        ld_cast(outbrow[:], din["outbrow"])
        ld_cast(wgt[:], din["wgt"])
        v_init_loads = sLX.n
        g_init_loads = sWG.n

        P.W("vector", sLX, v_init_loads)
        P.W("vector", sWG, g_init_loads)
        P.E("vector", "tensor_scalar_mul", qkvb[:, 0:8], qkvb[:, 0:8], 0.125)
        P.E("vector", "tensor_scalar_mul", rqb[:], rqb[:], 0.125)
        P.E("vector", "memset", onesrow[:], 1.0)
        P.E("vector", "memset", seps[:], 1e-5)
        P.E("vector", "memset", seps30[:], 1e-30)
        P.E("vector", "memset", sone1[:], 1.0)
        P.E("vector", "memset", seps6[:], 1e-6)
        P.D("vector")
        P.E("vector", "engine_nop", inc=(sV, 1))
        v_const_ready = sV.bump()

        with ExitStack() as esA:
            EA = esA.enter_context
            wqkv = EA(nc.sbuf_tensor("sb_wqkv", [128, 8 * 3 * D], BF16))
            wrq = EA(nc.sbuf_tensor("sb_wrq", [128, 8 * LATD], BF16))
            wrkv = EA(nc.sbuf_tensor("sb_wrkv", [128, 8 * 2 * LATD], BF16))
            wrout = EA(nc.sbuf_tensor("sb_wrout", [128, 2 * D], BF16))
            wout = EA(nc.sbuf_tensor("sb_wout", [128, 8 * D], BF16))

            sx = [EA(nc.sbuf_tensor(f"sx{i}", [128, NTA * D], F32))
                  for i in range(2)]
            sln1 = EA(nc.sbuf_tensor("sln1", [128, NTA * D], BF16))
            sln1f = EA(nc.sbuf_tensor("sln1f", [128, 8 * CHA], BF16))
            sq = EA(nc.sbuf_tensor("sq", [128, 8 * CHA], BF16))
            sk = EA(nc.sbuf_tensor("sk", [128, 8 * (CHA + 128)], BF16))
            sv_ = EA(nc.sbuf_tensor("sv", [128, (NTA + 1) * D], BF16))
            srq = EA(nc.sbuf_tensor("srq", [128, 2 * CHA], BF16))
            sbn = EA(nc.sbuf_tensor("sbn", [128, NTA * 12], F32))
            sbm = EA(nc.sbuf_tensor("sbm", [128, NTA * 2], F32))
            srstd = EA(nc.sbuf_tensor("srstd", [128, NTA], F32))
            sp = EA(nc.sbuf_tensor("sp", [128, 4 * 1024], BF16))
            spT = EA(nc.sbuf_tensor("spT", [128, 2 * 1024], BF16))
            snmax = EA(nc.sbuf_tensor("snmax", [128, 16], F32))
            ssum = EA(nc.sbuf_tensor("ssum", [128, 16], F32))
            srcl = EA(nc.sbuf_tensor("srcl", [128, 16], F32))
            sloc = EA(nc.sbuf_tensor("sloc", [128, 8 * CHA], BF16))
            spoolT = EA(nc.sbuf_tensor("spoolT", [128, 8 * NLC], BF16))
            srk = EA(nc.sbuf_tensor("srk", [128, 2 * NLAT], BF16))
            srvf = EA(nc.sbuf_tensor("srvf", [128, 2 * NLAT], BF16))
            srv = EA(nc.sbuf_tensor("srv", [64, LATD], BF16))
            slp = EA(nc.sbuf_tensor("slp", [128, LHN * NLAT], BF16))
            slpT = EA(nc.sbuf_tensor("slpT", [64, LHN * 128], BF16))
            slnmax = EA(nc.sbuf_tensor("slnmax", [128, LHN], F32))
            slsum = EA(nc.sbuf_tensor("slsum", [128, LHN], F32))
            slrcl = EA(nc.sbuf_tensor("slrcl", [128, LHN], F32))
            slat = EA(nc.sbuf_tensor("slat", [128, 2 * CHA], BF16))
            srem = EA(nc.sbuf_tensor("srem", [128, 8 * CHA], BF16))
            sg1 = EA(nc.sbuf_tensor("sg1", [16, CHA], F32))
            sgbf = EA(nc.sbuf_tensor("sgbf", [16, CHA], BF16))
            sdiff = EA(nc.sbuf_tensor("sdiff", [128, 512], BF16))
            smx1 = EA(nc.sbuf_tensor("smx1", [128, 512], BF16))

            for dc in range(8):
                ld_cast(wqkv[:, dc * 3 * D:(dc + 1) * 3 * D],
                        din["wqkv"][dc * 128:(dc + 1) * 128, :], sWA)
            for dc in range(8):
                ld_cast(wrq[:, dc * LATD:(dc + 1) * LATD],
                        din["wrq"][dc * 128:(dc + 1) * 128, :], sWA)
                ld_cast(wrkv[:, dc * 2 * LATD:(dc + 1) * 2 * LATD],
                        din["wrkv"][dc * 128:(dc + 1) * 128, :], sWA)
                ld_cast(wout[:, dc * D:(dc + 1) * D],
                        din["wout"][dc * 128:(dc + 1) * 128, :], sWA)
            for dc in range(2):
                ld_cast(wrout[:, dc * D:(dc + 1) * D],
                        din["wrout"][dc * 128:(dc + 1) * 128, :], sWA)
            g_wa_loads = sWA.n

            def x_load(b, ch, waitv):
                par = ch % 2
                src = x_d[b * S + ch * CHA: b * S + (ch + 1) * CHA, :]
                if waitv is not None:
                    P.W("sync", waitv[0], waitv[1])
                P.E("sync", "dma_start",
                    out=sx[par][:].rearrange("p (t d) -> p t d", d=D),
                    in_=src.rearrange("(t p) d -> p t d", p=128),
                    inc=(sX[par], 16))
                return (sX[par], sX[par].bump(16))

            xload_done = {}
            xbuf_free = {0: None, 1: None}
            xload_done[(0, 0)] = x_load(0, 0, None)

            rot = {"slot": 0, "free": [None, None]}

            def proj_group(mms, evac, extra=()):
                slot = rot["slot"]
                rot["slot"] ^= 1
                bank = pA if slot == 0 else pB
                w = [x for x in extra if x is not None]
                if rot["free"][slot] is not None:
                    w.append(rot["free"][slot])
                n = len(mms)
                for i, (lh, rh) in enumerate(mms):
                    P.E("tensor", "matmul", bank[:, 0:rh.shape[-1]], lh, rh,
                        start=(i == 0), stop=(i == n - 1),
                        waits=(w if i == 0 else ()),
                        inc=((sT, 1) if i == n - 1 else None))
                tv = sT.bump()
                ev = evac(bank, tv)
                rot["free"][slot] = ev
                return ev

            def sc_evac(dst_ap, bias_ap, scale, width=512):
                def ev(bank, tv):
                    P.E("scalar", "activation", dst_ap, bank[:, 0:width],
                        ACTF.Identity, bias=bias_ap, scale=scale,
                        waits=[(sT, tv)], inc=(sS, 1))
                    return (sS, sS.bump())
                return ev

            def v_evac(dst_ap, width=512):
                def ev(bank, tv):
                    P.E("vector", "tensor_copy", dst_ap, bank[:, 0:width],
                        waits=[(sT, tv)], inc=(sV, 1))
                    return (sV, sV.bump())
                return ev

            pt1_free = [None]
            s_exp_last = None
            halo_prev = None
            ln1_done = {}

            def emit_ln1(cx, xv, guard=None):
                P.W("vector", xv[0], xv[1])
                if guard is not None:
                    P.W("vector", guard[0], guard[1])
                for ti in range(NTA):
                    P.E("vector", "bn_stats", sbn[:, ti * 12:ti * 12 + 6],
                        cx[:, ti * D:ti * D + 512])
                    P.E("vector", "bn_stats",
                        sbn[:, ti * 12 + 6:ti * 12 + 12],
                        cx[:, ti * D + 512:(ti + 1) * D])
                P.D("vector")
                for ti in range(NTA):
                    P.E("vector", "bn_aggr", sbm[:, ti * 2:ti * 2 + 2],
                        sbn[:, ti * 12:ti * 12 + 12])
                P.D("vector")
                P.E("vector", "engine_nop", inc=(sV, 1))
                v_stats = sV.bump()
                var_ap = sbm[:].rearrange("p (t c) -> p t c", c=2)[:, :, 1]
                P.E("scalar", "activation", srstd[:], var_ap, ACTF.Ln,
                    bias=seps[:, 0:1], scale=1.0, waits=[(sV, v_stats)])
                P.D("scalar")
                P.E("scalar", "activation", srstd[:], srstd[:], ACTF.Exp,
                    bias=0.0, scale=-0.5, inc=(sS, 1))
                s_rstd = sS.bump()
                mean_ap = sbm[:].rearrange("p (t c) -> p t c", c=2)[:, :, 0]
                P.W("vector", sS, s_rstd)
                for ti in range(NTA):
                    P.E("vector", "tensor_scalar",
                        sln1[:, ti * D:(ti + 1) * D],
                        cx[:, ti * D:(ti + 1) * D],
                        mean_ap[:, ti:ti + 1], srstd[:, ti:ti + 1],
                        ALU.subtract, ALU.mult)
                P.D("vector")
                P.E("vector", "engine_nop", inc=(sV, 1))
                return sV.bump()

            for b in range(NSEQ):
                P.W("vector", sWA, g_wa_loads)
                P.W("vector", sV, v_const_ready)
                P.E("vector", "memset", srk[:], 0.0)
                P.E("vector", "memset", srv[:], 0.0)
                P.E("vector", "memset", srvf[:], 0.0)
                P.E("vector", "memset",
                    sk[:].rearrange("p (c k) -> p c k", c=8)[:, :, 0:128], 0.0)
                P.E("vector", "memset", sv_[:, 0:D], 0.0)
                P.D("vector")
                P.E("vector", "engine_nop", inc=(sV, 1))
                seq_reset = sV.bump()

                for ch in range(NCH_A):
                    xv = xload_done[(b, ch)]
                    nb, nch = (b, ch + 1) if ch + 1 < NCH_A else (b + 1, 0)
                    if nb < NSEQ:
                        xload_done[(nb, nch)] = x_load(
                            nb, nch, xbuf_free[(ch + 1) % 2])
                    cx = sx[ch % 2]

                    if (b, ch) in ln1_done:
                        v_ln1 = ln1_done[(b, ch)]
                    else:
                        v_ln1 = emit_ln1(cx, xv)

                    if LEVEL < 2:
                        continue
                    l0 = ch * NLC
                    w0 = [(sV, v_ln1), (sS, sS.n), halo_prev,
                          rot["free"][0], rot["free"][1]]
                    if ch == 0 and b == 0:
                        w0 += [(sWA, g_wa_loads), (sLX, v_init_loads)]
                    for dc in range(8):
                        for ti in range(NTA):
                            last = (dc == 7 and ti == NTA - 1)
                            P.E("tensor", "matmul",
                                pG[:, dc * NLC + ti * LPT:
                                   dc * NLC + (ti + 1) * LPT],
                                sln1[:, ti * D + dc * 128:
                                     ti * D + (dc + 1) * 128],
                                indp[:],
                                start=True, stop=True,
                                waits=(w0 if (dc == 0 and ti == 0) else ()),
                                inc=((sT, 1) if last else None))
                    t_pool = sT.bump()
                    P.E("vector", "tensor_copy", spoolT[:, 0:8 * NLC],
                        pG[:, 0:8 * NLC], waits=[(sT, t_pool)],
                        inc=(sV, 1))
                    v_poolT = sV.bump()

                    for jc in range(4):
                        mms = [(wrkv[:, dc * 2 * LATD + jc * 128:
                                     dc * 2 * LATD + (jc + 1) * 128],
                                spoolT[:, dc * NLC:(dc + 1) * NLC])
                               for dc in range(8)]
                        if jc < 2:
                            dst = srk[:, jc * NLAT + l0:jc * NLAT + l0 + NLC]
                        else:
                            dst = srvf[:, (jc - 2) * NLAT + l0:
                                       (jc - 2) * NLAT + l0 + NLC]
                        proj_group(
                            mms, sc_evac(dst, rkvb[:, jc:jc + 1], 1.0,
                                         width=NLC),
                            extra=([(sV, v_poolT)] if jc == 0 else ()))
                    rkv_done = sS.n
                    for pr in range(2):
                        P.E("tensor", "transpose",
                            pT1[0:NLAT, pr * 128:(pr + 1) * 128],
                            srvf[:, pr * NLAT:(pr + 1) * NLAT],
                            ident[:],
                            waits=([(sS, rkv_done), pt1_free[0]]
                                   if pr == 0 else ()),
                            inc=((sT, 1) if pr == 1 else None))
                    t_rvT = sT.bump()
                    P.E("vector", "tensor_copy", srv[0:NLAT, 0:256],
                        pT1[0:NLAT, 0:256], waits=[(sT, t_rvT)],
                        inc=(sV, 1))
                    v_lat_w = sV.bump()
                    pt1_free[0] = (sV, v_lat_w)

                    if LEVEL < 3:
                        continue
                    tT_prev = v_lat_w
                    for ti in range(NTA):
                        for dc in range(8):
                            P.E("tensor", "transpose",
                                pT1[:, dc * 128:(dc + 1) * 128],
                                sln1[:, ti * D + dc * 128:
                                     ti * D + (dc + 1) * 128],
                                ident[:],
                                waits=(([(sV, tT_prev), pt1_free[0]])
                                       if dc == 0 else ()),
                                inc=((sT, 1) if dc == 7 else None))
                        tv = sT.bump()
                        dst = bass.AP(sln1f, ti * 128,
                                      [[8 * CHA, 128], [CHA, 8], [1, 128]])
                        P.E("vector", "tensor_copy", dst,
                            pT1[:].rearrange("p (c k) -> p c k", c=8),
                            waits=[(sT, tv)], inc=(sV, 1))
                        tT_prev = sV.bump()
                        pt1_free[0] = (sV, tT_prev)
                    v_ln1f = tT_prev
                    t_a3 = sT.n

                    if LEVEL < 4:
                        continue
                    extra0 = [(sV, v_ln1f)]
                    for jc in range(8):
                        mms = [(wqkv[:, dc * 3 * D + jc * 128:
                                     dc * 3 * D + (jc + 1) * 128],
                                sln1f[:, dc * CHA:dc * CHA + 512])
                               for dc in range(8)]
                        proj_group(mms,
                                   sc_evac(sq[:, jc * CHA:jc * CHA + 512],
                                           qkvb[:, jc:jc + 1], 0.125),
                                   extra=extra0)
                        extra0 = ()
                    for jc in range(8):
                        mms = [(wqkv[:, dc * 3 * D + D + jc * 128:
                                     dc * 3 * D + D + (jc + 1) * 128],
                                sln1f[:, dc * CHA:dc * CHA + 512])
                               for dc in range(8)]
                        proj_group(mms,
                                   sc_evac(sk[:, jc * (CHA + 128) + 128:
                                              jc * (CHA + 128) + 640],
                                           qkvb[:, 8 + jc:9 + jc], 1.0))
                    for ti in range(NTA):
                        for jh in range(2):
                            mms = [(sln1f[:, dc * CHA + ti * 128:
                                          dc * CHA + (ti + 1) * 128],
                                    wqkv[:, dc * 3 * D + 2 * D + jh * 512:
                                         dc * 3 * D + 2 * D + (jh + 1) * 512])
                                   for dc in range(8)]
                            mms.append((onesrow[0:1, 0:128],
                                        vbrow[0:1, jh * 512:(jh + 1) * 512]))
                            proj_group(mms, v_evac(
                                sv_[:, (1 + ti) * D + jh * 512:
                                    (1 + ti) * D + (jh + 1) * 512]))
                    for jc in range(2):
                        mms = [(wrq[:, dc * LATD + jc * 128:
                                    dc * LATD + (jc + 1) * 128],
                                sln1f[:, dc * CHA:dc * CHA + 512])
                               for dc in range(8)]
                        proj_group(mms,
                                   sc_evac(srq[:, jc * CHA:jc * CHA + 512],
                                           rqb[:, jc:jc + 1], 0.125))
                    qkv_done_s = sS.n
                    qkv_done_v = sV.n
                    if nb < NSEQ:
                        ln1_done[(nb, nch)] = emit_ln1(
                            sx[nch % 2], xload_done[(nb, nch)],
                            guard=(sT, t_a3))

                    if LEVEL < 5:
                        continue
                    st5 = {
                        "slotfree": {0: s_exp_last, 1: s_exp_last},
                        "sp_free": {0: None, 1: None, 2: None, 3: None},
                        "loc_free": rot["free"][0],
                        "first5": [(sV, qkv_done_v)],
                        "ctx": {},
                        "lat_prev": None,
                    }

                    def a5_front(idx, ti, g):
                        slot = g % 2
                        sp4 = idx % 4
                        bankS = pC if slot == 0 else pD
                        w = [(sS, qkv_done_s)] + st5["first5"]
                        st5["first5"] = []
                        if st5["slotfree"][slot] is not None:
                            w.append(st5["slotfree"][slot])
                        for i_, s_ in enumerate((0, 2, 1, 3)):
                            h = g * 4 + s_
                            jc = h // 2
                            r0 = 64 * (h % 2)
                            c0 = (h % 2) * 512 + ((h // 2) % 2) * 256
                            P.E("tensor", "matmul",
                                bankS[:, c0:c0 + 256],
                                sq[r0:r0 + 64, jc * CHA + ti * 128:
                                   jc * CHA + (ti + 1) * 128],
                                sk[r0:r0 + 64, jc * (CHA + 128) + ti * 128:
                                   jc * (CHA + 128) + ti * 128 + 256],
                                start=True, stop=True,
                                waits=(w if i_ == 0 else ()),
                                inc=((sT, 1) if i_ == 3 else None))
                        t_sc = sT.bump()
                        P.E("vector", "reduce_max",
                            snmax[:, g * 4:(g + 1) * 4],
                            bankS[:].rearrange("p (s k) -> p s k", s=4),
                            axis=AXX, negate=True,
                            waits=[(sT, t_sc)])
                        P.D("vector")
                        P.E("vector", "engine_nop", inc=(sV, 1))
                        v_msk = sV.bump()
                        wsp = [(sV, v_msk), st5["sp_free"][sp4]]
                        for s_ in range(4):
                            c0 = (s_ % 2) * 512 + ((s_ // 2) % 2) * 256
                            j_ = (s_ % 2) * 2 + s_ // 2
                            P.E("scalar", "activation",
                                sp[:, sp4 * 1024 + s_ * 256:
                                   sp4 * 1024 + (s_ + 1) * 256],
                                bankS[:, c0:c0 + 256], ACTF.Exp,
                                bias=snmax[:, g * 4 + j_:g * 4 + j_ + 1],
                                scale=1.0,
                                waits=(wsp if s_ == 0 else ()),
                                inc=((sS, 1) if s_ == 3 else None))
                        s_exp = sS.bump()
                        st5["slotfree"][slot] = (sS, s_exp)
                        st5["ctx"][idx] = s_exp

                    def a5_back1(idx, ti, g):
                        sp4 = idx % 4
                        s_exp = st5["ctx"][idx]
                        mrow = 1 if (ch == 0 and ti == 0) else 0
                        for s_ in range(4):
                            P.E("vector", "scalar_tensor_tensor",
                                sp[:, sp4 * 1024 + s_ * 256:
                                   sp4 * 1024 + (s_ + 1) * 256],
                                sp[:, sp4 * 1024 + s_ * 256:
                                   sp4 * 1024 + (s_ + 1) * 256],
                                1.0,
                                bass.AP(mloc, mrow * 256,
                                        [[512, 128], [1, 256]]),
                                ALU.mult, ALU.mult,
                                accum_out=ssum[:, g * 4 + s_:g * 4 + s_ + 1],
                                waits=([(sS, s_exp)] if s_ == 0 else ()),
                                inc=((sV, 1) if s_ == 3 else None))
                        v_pm = sV.bump()
                        P.E("scalar", "activation",
                            srcl[:, g * 4:(g + 1) * 4],
                            ssum[:, g * 4:(g + 1) * 4], ACTF.Ln,
                            bias=seps30[:, 0:1], scale=1.0,
                            waits=[(sV, v_pm)])
                        P.D("scalar")
                        P.E("scalar", "activation",
                            srcl[:, g * 4:(g + 1) * 4],
                            srcl[:, g * 4:(g + 1) * 4], ACTF.Exp,
                            bias=0.0, scale=-1.0, inc=(sS, 1))
                        s_rc = sS.bump()
                        P.W("vector", sS, s_rc)
                        for s_ in range(4):
                            P.E("vector", "tensor_scalar_mul",
                                sp[:, sp4 * 1024 + s_ * 256:
                                   sp4 * 1024 + (s_ + 1) * 256],
                                sp[:, sp4 * 1024 + s_ * 256:
                                   sp4 * 1024 + (s_ + 1) * 256],
                                srcl[:, g * 4 + s_:g * 4 + s_ + 1],
                                inc=((sV, 1) if s_ == 3 else None))
                        st5["ctx"][(idx, "norm")] = sV.bump()

                    def a5_back2(idx, ti, g):
                        slot = g % 2
                        sp4 = idx % 4
                        g_p = st5["ctx"][(idx, "norm")]
                        wtr = [(sV, g_p), pt1_free[0]]
                        for s_ in range(4):
                            for kc in range(2):
                                P.E("tensor", "transpose",
                                    pT1[:, s_ * 256 + kc * 128:
                                        s_ * 256 + (kc + 1) * 128],
                                    sp[:, sp4 * 1024 + s_ * 256 + kc * 128:
                                       sp4 * 1024 + s_ * 256 +
                                       (kc + 1) * 128],
                                    ident[:],
                                    waits=(wtr if (s_ == 0 and kc == 0)
                                           else ()),
                                    inc=((sT, 1) if (s_ == 3 and kc == 1)
                                         else None))
                        t_ptr = sT.bump()
                        st5["sp_free"][sp4] = (sT, t_ptr)
                        P.E("scalar", "copy",
                            spT[:, slot * 1024:(slot + 1) * 1024],
                            pT1[:, 0:1024], waits=[(sT, t_ptr)],
                            inc=(sS, 1))
                        s_pT = sS.bump()
                        pt1_free[0] = (sS, s_pT)
                        wpv = [(sS, s_pT), st5["loc_free"]]
                        for s_ in range(4):
                            h = g * 4 + s_
                            pr = s_ // 2
                            r0 = 64 * (h % 2)
                            for kc in range(2):
                                P.E("tensor", "matmul",
                                    pA[r0:r0 + 64, pr * 128:(pr + 1) * 128],
                                    sv_[:, (ti + kc) * D + h * 64:
                                        (ti + kc) * D + (h + 1) * 64],
                                    spT[:, slot * 1024 + s_ * 256 + kc * 128:
                                        slot * 1024 + s_ * 256 +
                                        (kc + 1) * 128],
                                    start=(kc == 0), stop=(kc == 1),
                                    waits=(wpv if (s_ == 0 and kc == 0)
                                           else ()),
                                    inc=((sT, 1) if (s_ == 3 and kc == 1)
                                         else None))
                        t_pv = sT.bump()
                        dstl = bass.AP(sloc, (2 * g) * CHA + ti * 128,
                                       [[8 * CHA, 128], [CHA, 2], [1, 128]])
                        P.E("scalar", "copy", dstl,
                            pA[:, 0:256].rearrange("p (a k) -> p a k", a=2),
                            waits=[(sT, t_pv)], inc=(sS, 1))
                        st5["loc_free"] = (sS, sS.bump())
                        rot["free"][0] = st5["loc_free"]

                    def a6(ti):
                        tg = ch * NTA + ti
                        wl = [(sS, qkv_done_s), (sV, v_lat_w),
                              st5["lat_prev"],
                              rot["free"][1] if ti == 0 else None]
                        for lh in range(4):
                            r0 = 64 * (lh % 2)
                            jc = lh // 2
                            pb_ = pG if lh % 2 == 0 else pB
                            P.E("tensor", "matmul",
                                pb_[:, (lh // 2) * 64:(lh // 2 + 1) * 64],
                                srq[r0:r0 + 64, jc * CHA + ti * 128:
                                    jc * CHA + (ti + 1) * 128],
                                srk[r0:r0 + 64, jc * NLAT:(jc + 1) * NLAT],
                                start=True, stop=True,
                                waits=(wl if lh == 0 else ()),
                                inc=((sT, 1) if lh == 3 else None))
                        t_lsc = sT.bump()
                        for pi, pb_ in enumerate((pG, pB)):
                            P.E("vector", "reduce_max",
                                slnmax[:, pi * 2:(pi + 1) * 2],
                                pb_[:, 0:128].rearrange(
                                    "p (s k) -> p s k", s=2),
                                axis=AXX, negate=True,
                                waits=([(sT, t_lsc)] if pi == 0 else ()))
                        P.D("vector")
                        P.E("vector", "engine_nop", inc=(sV, 1))
                        v_lm = sV.bump()
                        for lh in range(4):
                            pb_ = pG if lh % 2 == 0 else pB
                            j_ = (lh % 2) * 2 + lh // 2
                            P.E("scalar", "activation",
                                slp[:, lh * NLAT:(lh + 1) * NLAT],
                                pb_[:, (lh // 2) * 64:(lh // 2 + 1) * 64],
                                ACTF.Exp,
                                bias=slnmax[:, j_:j_ + 1], scale=1.0,
                                waits=([(sV, v_lm)] if lh == 0 else ()),
                                inc=((sS, 1) if lh == 3 else None))
                        s_lexp = sS.bump()
                        rot["free"][1] = (sS, s_lexp)
                        for lh in range(4):
                            P.E("vector", "scalar_tensor_tensor",
                                slp[:, lh * NLAT:(lh + 1) * NLAT],
                                slp[:, lh * NLAT:(lh + 1) * NLAT],
                                1.0,
                                bass.AP(mlat, 64 - LPT * tg,
                                        [[128, 128], [1, NLAT]]),
                                ALU.mult, ALU.mult,
                                accum_out=slsum[:, lh:lh + 1],
                                waits=([(sS, s_lexp)] if lh == 0 else ()),
                                inc=((sV, 1) if lh == 3 else None))
                        v_lpm = sV.bump()
                        P.E("scalar", "activation", slrcl[:], slsum[:],
                            ACTF.Ln, bias=seps30[:, 0:1], scale=1.0,
                            waits=[(sV, v_lpm)])
                        P.D("scalar")
                        P.E("scalar", "activation", slrcl[:], slrcl[:],
                            ACTF.Exp, bias=0.0, scale=-1.0, inc=(sS, 1))
                        s_lrc = sS.bump()
                        P.W("vector", sS, s_lrc)
                        for lh in range(4):
                            P.E("vector", "tensor_scalar_mul",
                                slp[:, lh * NLAT:(lh + 1) * NLAT],
                                slp[:, lh * NLAT:(lh + 1) * NLAT],
                                slrcl[:, lh:lh + 1],
                                inc=((sV, 1) if lh == 3 else None))
                        g_lp = sV.bump()
                        wlt = [(sV, g_lp), pt1_free[0]]
                        for lh in range(4):
                            P.E("tensor", "transpose",
                                pT1[0:NLAT, lh * 128:(lh + 1) * 128],
                                slp[:, lh * NLAT:(lh + 1) * NLAT],
                                ident[:],
                                waits=(wlt if lh == 0 else ()),
                                inc=((sT, 1) if lh == 3 else None))
                        t_lT = sT.bump()
                        P.E("vector", "tensor_copy", slpT[:],
                            pT1[0:NLAT, 0:512], waits=[(sT, t_lT)],
                            inc=(sV, 1))
                        v_lT = sV.bump()
                        pt1_free[0] = (sV, v_lT)
                        for lh in range(4):
                            pr = lh // 2
                            r0 = 64 * (lh % 2)
                            P.E("tensor", "matmul",
                                pG[r0:r0 + 64,
                                   256 + pr * 128:256 + (pr + 1) * 128],
                                srv[0:NLAT, lh * 64:(lh + 1) * 64],
                                slpT[0:NLAT, lh * 128:(lh + 1) * 128],
                                start=True, stop=True,
                                waits=([(sV, v_lT)] if lh == 0 else ()),
                                inc=((sT, 1) if lh == 3 else None))
                        t_lpv = sT.bump()
                        dstL = bass.AP(slat, ti * 128,
                                       [[2 * CHA, 128], [CHA, 2], [1, 128]])
                        P.E("vector", "tensor_copy", dstL,
                            pG[:, 256:512].rearrange("p (a k) -> p a k", a=2),
                            waits=[(sT, t_lpv)], inc=(sV, 1))
                        st5["lat_prev"] = (sV, sV.bump())

                    items = [(ti_, g_) for ti_ in range(NTA)
                             for g_ in range(4)]
                    for idx, (ti_, g_) in enumerate(items):
                        a5_front(idx, ti_, g_)
                        if idx >= 1:
                            a5_back1(idx - 1, *items[idx - 1])
                        if idx >= 2:
                            a5_back2(idx - 2, *items[idx - 2])
                        if g_ == 3 and LEVEL >= 6:
                            a6(ti_)
                    n_it = len(items)
                    a5_back1(n_it - 1, *items[n_it - 1])
                    a5_back2(n_it - 2, *items[n_it - 2])
                    a5_back2(n_it - 1, *items[n_it - 1])
                    lat_prev = st5["lat_prev"]
                    loc_free = st5["loc_free"]

                    lat_done = lat_prev

                    P.W("vector", sT, sT.n)
                    P.E("vector", "tensor_copy",
                        sk[:].rearrange("p (c k) -> p c k", c=8)[:, :, 0:128],
                        sk[:].rearrange("p (c k) -> p c k",
                                        c=8)[:, :, CHA:CHA + 128])
                    P.E("vector", "tensor_copy", sv_[:, 0:D],
                        sv_[:, NTA * D:(NTA + 1) * D])
                    P.D("vector")
                    P.E("vector", "engine_nop", inc=(sV, 1))
                    halo_prev = (sV, sV.bump())

                    if LEVEL < 7:
                        continue
                    extra0 = [lat_done]
                    for jc in range(8):
                        mms = [(wrout[:, dc * D + jc * 128:
                                      dc * D + (jc + 1) * 128],
                                slat[:, dc * CHA:dc * CHA + 512])
                               for dc in range(2)]
                        proj_group(mms,
                                   sc_evac(srem[:, jc * CHA:jc * CHA + 512],
                                           routb[:, jc:jc + 1], 1.0),
                                   extra=extra0)
                        extra0 = ()
                    rem_done_s = sS.n

                    wg = [(sS, rem_done_s), lat_done]
                    for dc in range(8):
                        P.E("tensor", "matmul", pG[0:16, 0:512],
                            gstk[:, dc * 16:(dc + 1) * 16],
                            sloc[:, dc * CHA:dc * CHA + 512],
                            start=(dc == 0), stop=False,
                            waits=(wg if dc == 0 else ()))
                    for dc in range(8):
                        P.E("tensor", "matmul", pG[0:16, 0:512],
                            gstk[:, 128 + dc * 16:128 + (dc + 1) * 16],
                            srem[:, dc * CHA:dc * CHA + 512],
                            start=False, stop=(dc == 7),
                            inc=((sT, 1) if dc == 7 else None))
                    t_gate = sT.bump()
                    P.E("scalar", "activation", sg1[0:16, :],
                        pG[0:16, 0:512], ACTF.Exp, bias=0.0, scale=-1.0,
                        waits=[(sT, t_gate)], inc=(sS, 1))
                    s_g1 = sS.bump()
                    P.D("scalar")
                    P.E("scalar", "activation", sg1[0:16, :], sg1[0:16, :],
                        ACTF.Ln, bias=sone1[0:16, 0:1], scale=1.0)
                    P.D("scalar")
                    P.E("scalar", "activation", sgbf[0:16, :], sg1[0:16, :],
                        ACTF.Exp, bias=0.0, scale=-1.0, inc=(sS, 1))
                    s_gate = sS.bump()
                    mix_prev = None
                    for dc in range(8):
                        w = [(sS, s_gate), mix_prev]
                        P.E("tensor", "matmul", pG[:, 0:512],
                            estk[0:16, dc * 128:(dc + 1) * 128],
                            sgbf[0:16, 0:512], start=True, stop=True,
                            waits=w, inc=(sT, 1))
                        t_ge = sT.bump()
                        P.E("vector", "tensor_tensor", sdiff[:],
                            sloc[:, dc * CHA:dc * CHA + 512],
                            srem[:, dc * CHA:dc * CHA + 512],
                            ALU.subtract, waits=[(sT, t_ge)])
                        P.D("vector")
                        P.E("vector", "tensor_tensor", smx1[:],
                            pG[:, 0:512], sdiff[:], ALU.mult)
                        P.D("vector")
                        P.E("vector", "tensor_tensor",
                            sln1f[:, dc * CHA:dc * CHA + 512],
                            smx1[:], srem[:, dc * CHA:dc * CHA + 512],
                            ALU.add, inc=(sV, 1))
                        mix_prev = (sV, sV.bump())
                    v_mixed = mix_prev[1]

                    if LEVEL < 8:
                        continue
                    extra0 = [(sV, v_mixed)]
                    ev = None
                    for ti in range(NTA):
                        for jh in range(2):
                            mms = [(sln1f[:, dc * CHA + ti * 128:
                                          dc * CHA + (ti + 1) * 128],
                                    wout[:, dc * D + jh * 512:
                                         dc * D + (jh + 1) * 512])
                                   for dc in range(8)]
                            mms.append((onesrow[0:1, 0:128],
                                        outbrow[0:1,
                                                jh * 512:(jh + 1) * 512]))

                            def mk_res_evac(ti=ti, jh=jh):
                                def evf(bank, tv):
                                    c0 = ti * D + jh * 512
                                    P.E("vector", "tensor_tensor",
                                        cx[:, c0:c0 + 512],
                                        bank[:, 0:512], cx[:, c0:c0 + 512],
                                        ALU.add, waits=[(sT, tv)],
                                        inc=(sV, 1))
                                    return (sV, sV.bump())
                                return evf

                            ev = proj_group(mms, mk_res_evac(), extra=extra0)
                            extra0 = ()
                    v_x2 = ev[1]
                    P.W("sync", sV, v_x2)
                    dst = x2_d[b * S + ch * CHA: b * S + (ch + 1) * CHA, :]
                    P.E("sync", "dma_start",
                        out=dst.rearrange("(t p) d -> p t d", p=128),
                        in_=cx[:].rearrange("p (t d) -> p t d", d=D),
                        inc=(sSX[ch % 2], 16))
                    xbuf_free[ch % 2] = (sSX[ch % 2], sSX[ch % 2].bump(16))

            a_stores = (sSX[0].n, sSX[1].n)
            P.flush()

        with ExitStack() as esB:
            EB = esB.enter_context
            sw1 = [EB(nc.sbuf_tensor(f"sw1_{i}", [128, 8 * HFF], BF16))
                   for i in range(3)]
            sw2 = [EB(nc.sbuf_tensor(f"sw2_{i}", [128, 8 * D], BF16))
                   for i in range(3)]
            sxb = [EB(nc.sbuf_tensor(f"sxb{i}", [128, NTB * D], F32))
                   for i in range(2)]
            sln2 = EB(nc.sbuf_tensor("sln2", [128, NTB * D], BF16))
            sln2f = EB(nc.sbuf_tensor("sln2f", [128, 2 * 8 * CHB], BF16))
            sbn2 = EB(nc.sbuf_tensor("sbn2", [128, NTB * 12], F32))
            sbm2 = EB(nc.sbuf_tensor("sbm2", [128, NTB * 2], F32))
            srstd2 = EB(nc.sbuf_tensor("srstd2", [128, NTB], F32))
            shid = [EB(nc.sbuf_tensor(f"shid{i}", [128, 8 * 512], BF16))
                    for i in range(2)]
            sacc = EB(nc.sbuf_tensor("sacc", [128, NTB * D], F32))
            slg = EB(nc.sbuf_tensor("slg", [128, 2 * (NTB * EE)], F32))
            sz = EB(nc.sbuf_tensor("sz", [128, 2 * (NTB * EE)], F32))
            sm12 = EB(nc.sbuf_tensor("sm12", [128, 2 * 2 * NTB], F32))
            sge = EB(nc.sbuf_tensor("sge", [128, 2 * (NTB * EE)], F32))
            swsel = EB(nc.sbuf_tensor("swsel", [128, 2 * (NTB * EE)], F32))
            swe = EB(nc.sbuf_tensor("swe", [128, 2 * (NTB * EE)], F32))
            ssum2 = EB(nc.sbuf_tensor("ssum2", [128, 2 * (NTB)], F32))
            srelu = EB(nc.sbuf_tensor("srelu", [128, 2 * 512], BF16))

            def wload(e, slot, after=(), after2=None):
                if e < EE:
                    src1, src2 = din["w1t"][e], din["w2t"][e]
                else:
                    src1, src2 = din["cfct"], din["cprjt"]
                for wv_ in after:
                    if wv_ is not None:
                        P.W("gpsimd", wv_[0], wv_[1])
                sm1, sm2 = sW[slot], sW2s[slot]
                P.E("gpsimd", "dma_start",
                    out=sw1[slot][:].rearrange("p (c k) -> p c k", c=8),
                    in_=src1.rearrange("(c p) k -> p c k", p=128),
                    inc=(sm1, 16))
                v1 = sm1.bump(16)
                if after2 is not None:
                    P.W("gpsimd", after2[0], after2[1])
                P.E("gpsimd", "dma_start",
                    out=sw2[slot][:].rearrange("p (c k) -> p c k", c=8),
                    in_=src2.rearrange("(c p) k -> p c k", p=128),
                    inc=(sm2, 16))
                return ((sm1, v1), (sm2, sm2.bump(16)))

            rotB = {"slot": 0, "free": [None, None]}

            def proj_groupB(pool, mms, evac, extra=()):
                slot = rotB["slot"]
                rotB["slot"] ^= 1
                bank = pool[slot]
                w = [x for x in extra if x is not None]
                if rotB["free"][slot] is not None:
                    w.append(rotB["free"][slot])
                n = len(mms)
                for i, (lh, rh) in enumerate(mms):
                    P.E("tensor", "matmul", bank[:, 0:rh.shape[-1]], lh, rh,
                        start=(i == 0), stop=(i == n - 1),
                        waits=(w if i == 0 else ()),
                        inc=((sT, 1) if i == n - 1 else None))
                tv = sT.bump()
                ev = evac(bank, tv)
                rotB["free"][slot] = ev
                return ev

            xfree = {}
            out_stored = {}
            prep = {}
            tln2f = {}

            def prepB(ch):
                par = ch % 2
                cx = sxb[par]
                if ch == 0:
                    P.W("sync", sSX[0], a_stores[0])
                    P.W("sync", sSX[1], a_stores[1])
                if ch >= 2:
                    P.W("sync", sV, xfree[ch - 2])
                src = x2_d[ch * CHB:(ch + 1) * CHB, :]
                P.E("sync", "dma_start",
                    out=cx[:].rearrange("p (t d) -> p t d", d=D),
                    in_=src.rearrange("(t p) d -> p t d", p=128),
                    inc=(sX[par], 16))
                xv = (sX[par], sX[par].bump(16))

                P.W("vector", xv[0], xv[1])
                for ti in range(NTB):
                    P.E("vector", "bn_stats", sbn2[:, ti * 12:ti * 12 + 6],
                        cx[:, ti * D:ti * D + 512])
                    P.E("vector", "bn_stats",
                        sbn2[:, ti * 12 + 6:ti * 12 + 12],
                        cx[:, ti * D + 512:(ti + 1) * D])
                P.D("vector")
                for ti in range(NTB):
                    P.E("vector", "bn_aggr", sbm2[:, ti * 2:ti * 2 + 2],
                        sbn2[:, ti * 12:ti * 12 + 12])
                P.D("vector")
                P.E("vector", "engine_nop", inc=(sV, 1))
                v_stats = sV.bump()
                var_ap = sbm2[:].rearrange("p (t c) -> p t c", c=2)[:, :, 1]
                P.E("scalar", "activation", srstd2[:], var_ap, ACTF.Ln,
                    bias=seps[:, 0:1], scale=1.0, waits=[(sV, v_stats)])
                P.D("scalar")
                P.E("scalar", "activation", srstd2[:], srstd2[:], ACTF.Exp,
                    bias=0.0, scale=-0.5, inc=(sS, 1))
                s_rstd = sS.bump()
                mean_ap = sbm2[:].rearrange("p (t c) -> p t c", c=2)[:, :, 0]
                P.W("vector", sS, s_rstd)
                if ch >= 1:
                    P.W("vector", sT, tln2f[ch - 1])
                for ti in range(NTB):
                    P.E("vector", "tensor_scalar",
                        sln2[:, ti * D:(ti + 1) * D],
                        cx[:, ti * D:(ti + 1) * D],
                        mean_ap[:, ti:ti + 1], srstd2[:, ti:ti + 1],
                        ALU.subtract, ALU.mult)
                P.D("vector")
                P.E("vector", "engine_nop", inc=(sV, 1))
                v_ln2 = sV.bump()

                f0 = par * 8 * CHB
                tprev = v_ln2
                for ti in range(NTB):
                    for dc in range(8):
                        P.E("tensor", "transpose",
                            pT1[:, dc * 128:(dc + 1) * 128],
                            sln2[:, ti * D + dc * 128:
                                 ti * D + (dc + 1) * 128],
                            ident[:],
                            waits=([(sV, tprev)] if dc == 0 else ()),
                            inc=((sT, 1) if dc == 7 else None))
                    tv = sT.bump()
                    dst = bass.AP(sln2f, f0 + ti * 128,
                                  [[16 * CHB, 128], [CHB, 8], [1, 128]])
                    P.E("vector", "tensor_copy", dst,
                        pT1[:].rearrange("p (c k) -> p c k", c=8),
                        waits=[(sT, tv)], inc=(sV, 1))
                    tprev = sV.bump()
                v_ln2f = tprev
                tln2f[ch] = sT.n

                r0_ = par * NTB * EE
                m0_ = par * 2 * NTB
                for ti in range(NTB):
                    for dc in range(8):
                        P.E("tensor", "matmul",
                            pG[:, ti * EE:(ti + 1) * EE],
                            sln2f[:, f0 + dc * CHB + ti * 128:
                                  f0 + dc * CHB + (ti + 1) * 128],
                            wgt[:, dc * EE:(dc + 1) * EE],
                            start=(dc == 0), stop=(dc == 7),
                            waits=([(sV, v_ln2f)] if (ti == 0 and dc == 0)
                                   else ()),
                            inc=((sT, 1) if (ti == NTB - 1 and dc == 7)
                                 else None))
                t_lg = sT.bump()
                P.E("vector", "tensor_copy", slg[:, r0_:r0_ + NTB * EE],
                    pG[:, 0:NTB * EE], waits=[(sT, t_lg)])
                P.D("vector")
                P.E("vector", "tensor_tensor",
                    sz[:, r0_:r0_ + NTB * EE].rearrange(
                        "p (t e) -> p t e", e=EE),
                    slg[:, r0_:r0_ + NTB * EE].rearrange(
                        "p (t e) -> p t e", e=EE),
                    bass.AP(lb, 0, [[EE, 128], [0, NTB], [1, EE]]),
                    ALU.add)
                P.D("vector")
                P.E("vector", "reduce_max", sm12[:, m0_:m0_ + NTB],
                    sz[:, r0_:r0_ + NTB * EE].rearrange(
                        "p (t e) -> p t e", e=EE), axis=AXX)
                P.D("vector")
                for ti in range(NTB):
                    P.E("vector", "tensor_scalar",
                        sge[:, r0_ + ti * EE:r0_ + (ti + 1) * EE],
                        sz[:, r0_ + ti * EE:r0_ + (ti + 1) * EE],
                        sm12[:, m0_ + ti:m0_ + ti + 1], -1e30,
                        ALU.is_ge, ALU.mult)
                P.D("vector")
                P.E("vector", "tensor_tensor",
                    sge[:, r0_:r0_ + NTB * EE],
                    sz[:, r0_:r0_ + NTB * EE],
                    sge[:, r0_:r0_ + NTB * EE], ALU.add)
                P.D("vector")
                P.E("vector", "reduce_max",
                    sm12[:, m0_ + NTB:m0_ + 2 * NTB],
                    sge[:, r0_:r0_ + NTB * EE].rearrange(
                        "p (t e) -> p t e", e=EE), axis=AXX)
                P.D("vector")
                for ti in range(NTB):
                    P.E("vector", "tensor_scalar",
                        sge[:, r0_ + ti * EE:r0_ + (ti + 1) * EE],
                        sz[:, r0_ + ti * EE:r0_ + (ti + 1) * EE],
                        sm12[:, m0_ + NTB + ti:m0_ + NTB + ti + 1], None,
                        ALU.is_ge)
                P.D("vector")
                P.E("vector", "engine_nop", inc=(sV, 1))
                v_mask = sV.bump()
                P.E("scalar", "activation", swsel[:, r0_:r0_ + NTB * EE],
                    slg[:, r0_:r0_ + NTB * EE], ACTF.Exp,
                    bias=0.0, scale=-1.0, waits=[(sV, v_mask)])
                P.D("scalar")
                P.E("scalar", "activation", swsel[:, r0_:r0_ + NTB * EE],
                    swsel[:, r0_:r0_ + NTB * EE], ACTF.Ln,
                    bias=sone1[:, 0:1], scale=1.0)
                P.D("scalar")
                P.E("scalar", "activation", swsel[:, r0_:r0_ + NTB * EE],
                    swsel[:, r0_:r0_ + NTB * EE], ACTF.Exp,
                    bias=0.0, scale=-1.0, inc=(sS, 1))
                s_sig = sS.bump()
                P.E("vector", "tensor_tensor", swsel[:, r0_:r0_ + NTB * EE],
                    swsel[:, r0_:r0_ + NTB * EE],
                    sge[:, r0_:r0_ + NTB * EE],
                    ALU.mult, waits=[(sS, s_sig)])
                P.D("vector")
                P.E("vector", "reduce_sum", ssum2[:, par * NTB:
                                                  (par + 1) * NTB],
                    swsel[:, r0_:r0_ + NTB * EE].rearrange(
                        "p (t e) -> p t e", e=EE), axis=AXX,
                    inc=(sV, 1))
                v_sum2 = sV.bump()
                P.E("scalar", "activation",
                    ssum2[:, par * NTB:(par + 1) * NTB],
                    ssum2[:, par * NTB:(par + 1) * NTB], ACTF.Ln,
                    bias=seps6[:, 0:1], scale=1.0, waits=[(sV, v_sum2)])
                P.D("scalar")
                P.E("scalar", "activation",
                    ssum2[:, par * NTB:(par + 1) * NTB],
                    ssum2[:, par * NTB:(par + 1) * NTB], ACTF.Exp,
                    bias=0.0, scale=-1.0, inc=(sS, 1))
                s_rc2 = sS.bump()
                P.W("vector", sS, s_rc2)
                for ti in range(NTB):
                    P.E("vector", "tensor_scalar_mul",
                        swe[:, r0_ + ti * EE:r0_ + (ti + 1) * EE],
                        swsel[:, r0_ + ti * EE:r0_ + (ti + 1) * EE],
                        ssum2[:, par * NTB + ti:par * NTB + ti + 1])
                P.D("vector")
                P.E("vector", "engine_nop", inc=(sV, 1))
                return {"cx": cx, "v_ln2f": v_ln2f, "v_we": sV.bump(),
                        "par": par}

            nchb = NCH_B if LEVEL >= 9 else 0
            wq = {}
            if nchb > 0:
                prep[0] = prepB(0)
                aft0 = [(sT, sT.n),
                        (sSX[0], a_stores[0]), (sSX[1], a_stores[1])]
                for m in range(3):
                    wq[m] = wload(m % 9, m % 3,
                                  after=(aft0 if m == 0 else [(sT, sT.n)]))
            for ch in range(nchb):
                pp = prep[ch]
                cx = pp["cx"]
                par = pp["par"]
                v_ln2f = pp["v_ln2f"]
                v_we = pp["v_we"]
                f0 = par * 8 * CHB
                r0_ = par * NTB * EE

                ev = None
                vhid = {}
                t_hid_done = {}
                for stg in range(EE + 2):
                    if stg <= EE:
                        e = stg
                        m_ = ch * 9 + e
                        slot = m_ % 3
                        wv_e = wq[m_][0]
                        hid_last = None
                        for hc in range(8):
                            mms = [(sw1[slot][:, dc * HFF + hc * 128:
                                              dc * HFF + (hc + 1) * 128],
                                    sln2f[:, f0 + dc * CHB:
                                          f0 + dc * CHB + 512])
                                   for dc in range(8)]

                            def mk_hid_evac(hc=hc, e=e):
                                def evf(bank, tv):
                                    pr_ = hc % 2
                                    P.E("scalar", "activation",
                                        srelu[:, pr_ * 512:(pr_ + 1) * 512],
                                        bank[:, 0:512], ACTF.Relu,
                                        bias=0.0, scale=1.0,
                                        waits=[(sT, tv)], inc=(sS, 1))
                                    sval = sS.bump()
                                    P.E("vector", "tensor_tensor",
                                        shid[e % 2][:, hc * 512:
                                                    (hc + 1) * 512],
                                        srelu[:, pr_ * 512:(pr_ + 1) * 512],
                                        srelu[:, pr_ * 512:(pr_ + 1) * 512],
                                        ALU.mult,
                                        waits=[(sS, sval)], inc=(sV, 1))
                                    return (sV, sV.bump())
                                return evf

                            w0 = ([wv_e, (sV, v_ln2f)] if hc == 0 else [])
                            hid_last = proj_groupB((pA, pB), mms,
                                                   mk_hid_evac(), extra=w0)
                        vhid[e] = hid_last[1]
                        t_hid_done[m_] = sT.n
                    if stg == 6 and ch + 1 < NCH_B:
                        prep[ch + 1] = prepB(ch + 1)
                    if stg >= 1:
                        e = stg - 1
                        m_ = ch * 9 + e
                        slot = m_ % 3
                        for ti in range(NTB):
                            for jh in range(2):
                                mms = [(shid[e % 2][:, hc * 512 + ti * 128:
                                                    hc * 512 +
                                                    (ti + 1) * 128],
                                        sw2[slot][:, hc * D + jh * 512:
                                                  hc * D + (jh + 1) * 512])
                                       for hc in range(8)]

                                def mk_acc_evac(e=e, ti=ti, jh=jh):
                                    def evf(bank, tv):
                                        c0 = ti * D + jh * 512
                                        if e == 0:
                                            w_acc = [(sT, tv), (sV, v_we)]
                                            if (ti == 0 and jh == 0
                                                    and ch >= 1):
                                                w_acc.append(
                                                    (sST,
                                                     out_stored[ch - 1]))
                                            P.E("vector",
                                                "tensor_scalar_mul",
                                                sacc[:, c0:c0 + 512],
                                                bank[:, 0:512],
                                                swe[:, r0_ + ti * EE:
                                                    r0_ + ti * EE + 1],
                                                waits=w_acc,
                                                inc=(sV, 1))
                                        elif e < EE:
                                            P.E("vector",
                                                "scalar_tensor_tensor",
                                                sacc[:, c0:c0 + 512],
                                                bank[:, 0:512],
                                                swe[:, r0_ + ti * EE + e:
                                                    r0_ + ti * EE + e + 1],
                                                sacc[:, c0:c0 + 512],
                                                ALU.mult, ALU.add,
                                                waits=[(sT, tv)],
                                                inc=(sV, 1))
                                        else:
                                            P.E("vector", "tensor_tensor",
                                                sacc[:, c0:c0 + 512],
                                                bank[:, 0:512],
                                                sacc[:, c0:c0 + 512],
                                                ALU.add,
                                                waits=[(sT, tv)])
                                            P.D("vector")
                                            P.E("vector", "tensor_tensor",
                                                sacc[:, c0:c0 + 512],
                                                sacc[:, c0:c0 + 512],
                                                cx[:, c0:c0 + 512],
                                                ALU.add, inc=(sV, 1))
                                        return (sV, sV.bump())
                                    return evf

                                w0 = ([(sV, vhid[e]), wq[m_][1]]
                                      if (ti == 0 and jh == 0) else [])
                                ev = proj_groupB((pC, pD), mms,
                                                 mk_acc_evac(), extra=w0)
                        mp = m_ + 3
                        if mp < nchb * 9:
                            wq[mp] = wload(mp % 9, mp % 3,
                                           after=[(sT, t_hid_done[m_])],
                                           after2=(sT, sT.n))
                v_outc = ev[1]
                xfree[ch] = v_outc

                P.W("sync", sV, v_outc)
                dstO = out_d[ch * CHB:(ch + 1) * CHB, :]
                P.E("sync", "dma_start",
                    out=dstO.rearrange("(t p) d -> p t d", p=128),
                    in_=sacc[:].rearrange("p (t d) -> p t d", d=D),
                    inc=(sST, 16))
                out_stored[ch] = sST.bump(16)

            if sST.n:
                P.W("sync", sST, sST.n)
            P.flush()

    return nc



_S = 2048
_NSEQ = 2


def kernel(**inputs):
    consts = host_constants(_S)
    in_maps = []
    for core in range(NCORES):
        m = host_prep(inputs, core, _S, _NSEQ)
        m.update(consts)
        in_maps.append(m)
    nc = build(S=_S, NSEQ=_NSEQ, CHA=512, CHB=512)
    res = run_bass_kernel_spmd(nc, in_maps, core_ids=list(range(NCORES)))
    outs = [np.asarray(r["out"]) for r in res.results]
    return np.concatenate(outs, axis=0).reshape(16, _S, D)


if __name__ == "__main__":
    nc = build()
    n = nc.m.functions[0]
    print("blocks:", len(n.blocks))
    print("instructions:", sum(len(b.instructions) for b in n.blocks))


# version 11
# speedup vs baseline: 1.1119x; 1.1119x over previous
import numpy as np
import ml_dtypes
from contextlib import ExitStack

import concourse.bass as bass
import concourse.mybir as mybir
from concourse.bass_utils import run_bass_kernel_spmd

F32 = mybir.dt.float32
BF16 = mybir.dt.bfloat16
ALU = mybir.AluOpType
ACTF = mybir.ActivationFunctionType
AXX = mybir.AxisListType.X
BF = ml_dtypes.bfloat16

NEG = -1.0e9
D = 1024
NH = 16
HDIM = 64
WIN = 128
LATD = 256
LHN = 4
NLAT = 64
EE = 8
HFF = 1024
NCORES = 8


class Sem:
    def __init__(self, h):
        self.h = h
        self.n = 0

    def bump(self, k=1):
        self.n += k
        return self.n


class Plan:
    ENGS = ("sync", "gpsimd", "vector", "scalar", "tensor")

    def __init__(self, nc):
        self.nc = nc
        self.q = {k: [] for k in self.ENGS}

    def E(self, eng, meth, *args, waits=(), inc=None, **kw):
        waits = tuple(w for w in waits if w is not None)

        def f(e, meth=meth, args=args, kw=kw, waits=waits, inc=inc):
            for (s, v) in waits:
                e.wait_ge(s.h, v)
            r = getattr(e, meth)(*args, **kw)
            if inc is not None:
                r.then_inc(inc[0].h, inc[1])

        self.q[eng].append(f)

    def W(self, eng, sem, val):
        def f(e, sem=sem, val=val):
            e.wait_ge(sem.h, val)

        self.q[eng].append(f)

    def D(self, eng):
        def f(e):
            e.drain()

        self.q[eng].append(f)

    def flush(self):
        qs = self.q
        self.q = {k: [] for k in self.ENGS}
        with self.nc.Block() as blk:
            @blk.sync
            def _(e):
                for f in qs["sync"]:
                    f(e)

            @blk.gpsimd
            def _(e):
                for f in qs["gpsimd"]:
                    f(e)

            @blk.vector
            def _(e):
                for f in qs["vector"]:
                    f(e)

            @blk.scalar
            def _(e):
                for f in qs["scalar"]:
                    f(e)

            @blk.tensor
            def _(e):
                for f in qs["tensor"]:
                    f(e)



def host_constants(S):
    PC = S // NLAT
    LPT = 128 // PC
    c = {}
    c["ident"] = np.eye(128, dtype=BF)
    q = np.arange(128)[:, None]
    k = np.arange(256)[None, :]
    band = (k >= q) & (k <= q + 127)
    c["mloc"] = np.stack([band.astype(np.float32),
                          (band & (k >= 128)).astype(np.float32)]).astype(BF)
    r = np.arange(128)[:, None]
    u = np.arange(128)[None, :]
    c["mlat"] = ((u - 64) < (r // PC)).astype(np.float32).astype(BF)
    c["indp"] = (
        (r // PC == np.arange(LPT)[None, :]).astype(np.float32) / PC
    ).astype(BF)
    es = np.zeros((8, 16, 128), dtype=BF)
    for cc in range(8):
        for p in range(128):
            es[cc, 2 * cc + p // 64, p] = 1.0
    c["estk"] = np.ascontiguousarray(es.transpose(1, 0, 2).reshape(16, 1024))
    return c


def host_prep(inputs, core, S, NSEQ):
    A = np.ascontiguousarray
    ins = {}
    x = np.asarray(inputs["L0_attn_x"])
    ins["x"] = A(x[core * NSEQ:(core + 1) * NSEQ].reshape(NSEQ * S, D))
    ins["wqkv"] = A(np.asarray(inputs["L0_attn_qkv_w"]))
    ins["wrq"] = A(np.asarray(inputs["L0_attn_rq_w"]))
    ins["wrkv"] = A(np.asarray(inputs["L0_attn_rkv_w"]))
    ins["wrout"] = A(np.asarray(inputs["L0_attn_rout_w"]))
    ins["wout"] = A(np.asarray(inputs["L0_attn_out_w"]))
    qkvb = np.asarray(inputs["L0_attn_qkv_b"])
    ins["qkvb"] = A(qkvb[:2048].reshape(16, 128).T)
    ins["vbrow"] = A(qkvb[2048:].reshape(1, D))
    ins["rqb"] = A(np.asarray(inputs["L0_attn_rq_b"]).reshape(2, 128).T)
    ins["rkvb"] = A(np.asarray(inputs["L0_attn_rkv_b"]).reshape(4, 128).T)
    ins["routb"] = A(np.asarray(inputs["L0_attn_rout_b"]).reshape(8, 128).T)
    ins["outbrow"] = A(np.asarray(inputs["L0_attn_out_b"]).reshape(1, D))
    gl = np.asarray(inputs["L0_attn_gl_w"])[:, 0]
    gr = np.asarray(inputs["L0_attn_gr_w"])[:, 0]
    gstk = np.zeros((2, 8, 128, 16), dtype=np.float32)
    for cc in range(8):
        for half in range(2):
            h = 2 * cc + half
            gstk[0, cc, half * 64:(half + 1) * 64, h] = gl
            gstk[1, cc, half * 64:(half + 1) * 64, h] = gr
    ins["gstk"] = A(gstk.transpose(2, 0, 1, 3).reshape(128, 256))
    wgtT = np.asarray(inputs["L0_ffn_gate_w"]).T
    ins["wgt"] = A(wgtT.reshape(8, 128, EE).transpose(1, 0, 2).reshape(128, 64))
    ins["lb"] = A(np.tile(np.asarray(inputs["L0_ffn_lb_bias"]).reshape(1, EE),
                          (128, 1)))
    ins["w1t"] = A(np.asarray(inputs["L0_ffn_w1"]).transpose(0, 2, 1))
    ins["w2t"] = A(np.asarray(inputs["L0_ffn_w2"]).transpose(0, 2, 1))
    ins["cfct"] = A(np.asarray(inputs["L0_ffn_cfc"]).T)
    ins["cprjt"] = A(np.asarray(inputs["L0_ffn_cproj"]).T)
    return ins



LEVEL = 9
SUB5 = 8


def build(S=2048, NSEQ=2, CHA=512, CHB=512):
    PC = S // NLAT
    LPT = 128 // PC
    NTA = CHA // 128
    NCH_A = S // CHA
    NTB = CHB // 128
    NCH_B = (NSEQ * S) // CHB
    NLC = CHA // PC

    nc = bass.Bass("TRN2", target_bir_lowering=False, debug=False)
    dt = nc.dram_tensor

    x_d = dt("x", [NSEQ * S, D], F32, kind="ExternalInput").ap()
    out_d = dt("out", [NSEQ * S, D], F32, kind="ExternalOutput").ap()
    x2_d = dt("x2s", [NSEQ * S, D], F32, kind="Internal").ap()

    din = {}
    for name, shp in [
        ("wqkv", [D, 3 * D]), ("wrq", [D, LATD]), ("wrkv", [D, 2 * LATD]),
        ("wrout", [LATD, D]), ("wout", [D, D]), ("qkvb", [128, 16]),
        ("vbrow", [1, D]), ("rqb", [128, 2]), ("rkvb", [128, 4]),
        ("routb", [128, 8]), ("outbrow", [1, D]),
        ("gstk", [128, 256]), ("wgt", [128, 64]), ("lb", [128, EE]),
        ("w1t", [EE, D, HFF]), ("w2t", [EE, HFF, D]),
        ("cfct", [D, HFF]), ("cprjt", [HFF, D]),
    ]:
        din[name] = dt(name, shp, F32, kind="ExternalInput").ap()
    for name, shp in [("ident", [128, 128]), ("indp", [128, LPT]),
                      ("estk", [16, 1024]), ("mloc", [2, 128, 256]),
                      ("mlat", [128, 128])]:
        din[name] = dt(name, shp, BF16, kind="ExternalInput").ap()

    es = ExitStack()
    with es:
        EC = es.enter_context
        sLX = Sem(EC(nc.semaphore("sLX")))
        sWG = Sem(EC(nc.semaphore("sWG")))
        sX = [Sem(EC(nc.semaphore(f"sX{i}"))) for i in range(2)]
        sW = [Sem(EC(nc.semaphore(f"sW{i}"))) for i in range(3)]
        sW2s = [Sem(EC(nc.semaphore(f"sW2_{i}"))) for i in range(3)]
        sWA = Sem(EC(nc.semaphore("sWA")))
        sSX = [Sem(EC(nc.semaphore(f"sSX{i}"))) for i in range(2)]
        sT = Sem(EC(nc.semaphore("sT")))
        sV = Sem(EC(nc.semaphore("sV")))
        sS = Sem(EC(nc.semaphore("sS")))
        sST = Sem(EC(nc.semaphore("sST")))
        sG = Sem(EC(nc.semaphore("sG")))

        P = Plan(nc)

        pA = EC(nc.psum_tensor("pA", [128, 512], F32))
        pB = EC(nc.psum_tensor("pB", [128, 512], F32))
        pC = EC(nc.psum_tensor("pC", [128, 1024], F32))
        pD = EC(nc.psum_tensor("pD", [128, 1024], F32))
        pT1 = EC(nc.psum_tensor("pT1", [128, 1024], BF16))
        pG = EC(nc.psum_tensor("pG", [128, 512], F32))

        ident = EC(nc.sbuf_tensor("sb_ident", [128, 128], BF16))
        mloc = EC(nc.sbuf_tensor("sb_mloc", [128, 512], BF16))
        mlat = EC(nc.sbuf_tensor("sb_mlat", [128, 128], BF16))
        indp = EC(nc.sbuf_tensor("sb_indp", [128, LPT], BF16))
        estk = EC(nc.sbuf_tensor("sb_estk", [16, 1024], BF16))
        gstk = EC(nc.sbuf_tensor("sb_gstk", [128, 256], BF16))
        qkvb = EC(nc.sbuf_tensor("sb_qkvb", [128, 16], F32))
        rqb = EC(nc.sbuf_tensor("sb_rqb", [128, 2], F32))
        rkvb = EC(nc.sbuf_tensor("sb_rkvb", [128, 4], F32))
        routb = EC(nc.sbuf_tensor("sb_routb", [128, 8], F32))
        vbrow = EC(nc.sbuf_tensor("sb_vbrow", [1, D], BF16))
        outbrow = EC(nc.sbuf_tensor("sb_outbrow", [1, D], BF16))
        onesrow = EC(nc.sbuf_tensor("onesrow", [1, 512], BF16))
        wgt = EC(nc.sbuf_tensor("sb_wgt", [128, 8 * EE], BF16))
        seps = EC(nc.sbuf_tensor("sb_eps", [128, 1], F32))
        seps30 = EC(nc.sbuf_tensor("sb_eps30", [128, 1], F32))
        sone1 = EC(nc.sbuf_tensor("sb_one1", [128, 1], F32))
        seps6 = EC(nc.sbuf_tensor("sb_eps6", [128, 1], F32))
        lb = EC(nc.sbuf_tensor("sb_lb", [128, EE], F32))

        def ld_sync(dst_ap, src_ap):
            P.E("sync", "dma_start", out=dst_ap, in_=src_ap, inc=(sLX, 16))
            return sLX.bump(16)

        def ld_cast(dst_ap, src_ap, sm=None):
            sm = sm or sWG
            P.E("gpsimd", "dma_start", out=dst_ap, in_=src_ap, inc=(sm, 16))
            return sm.bump(16)

        ld_cast_dummy = None
        ld_sync(qkvb[:], din["qkvb"])
        ld_sync(rqb[:], din["rqb"])
        ld_sync(rkvb[:], din["rkvb"])
        ld_sync(routb[:], din["routb"])
        ld_sync(lb[:], din["lb"])
        ld_cast(ident[:], din["ident"])
        ld_cast(indp[:], din["indp"])
        ld_cast(estk[:], din["estk"])
        ld_cast(mloc[:].rearrange("p (a k) -> p a k", a=2),
                din["mloc"].rearrange("a p k -> p a k"))
        ld_cast(mlat[:], din["mlat"])
        ld_cast(gstk[:], din["gstk"])
        ld_cast(vbrow[:], din["vbrow"])
        ld_cast(outbrow[:], din["outbrow"])
        ld_cast(wgt[:], din["wgt"])
        v_init_loads = sLX.n
        g_init_loads = sWG.n

        P.W("vector", sLX, v_init_loads)
        P.W("vector", sWG, g_init_loads)
        P.E("vector", "tensor_scalar_mul", qkvb[:, 0:8], qkvb[:, 0:8], 0.125)
        P.E("vector", "tensor_scalar_mul", rqb[:], rqb[:], 0.125)
        P.E("vector", "memset", onesrow[:], 1.0)
        P.E("vector", "memset", seps[:], 1e-5)
        P.E("vector", "memset", seps30[:], 1e-30)
        P.E("vector", "memset", sone1[:], 1.0)
        P.E("vector", "memset", seps6[:], 1e-6)
        P.D("vector")
        P.E("vector", "engine_nop", inc=(sV, 1))
        v_const_ready = sV.bump()

        with ExitStack() as esA:
            EA = esA.enter_context
            wqkv = EA(nc.sbuf_tensor("sb_wqkv", [128, 8 * 3 * D], BF16))
            wrq = EA(nc.sbuf_tensor("sb_wrq", [128, 8 * LATD], BF16))
            wrkv = EA(nc.sbuf_tensor("sb_wrkv", [128, 8 * 2 * LATD], BF16))
            wrout = EA(nc.sbuf_tensor("sb_wrout", [128, 2 * D], BF16))
            wout = EA(nc.sbuf_tensor("sb_wout", [128, 8 * D], BF16))

            sx = [EA(nc.sbuf_tensor(f"sx{i}", [128, NTA * D], F32))
                  for i in range(2)]
            sln1 = EA(nc.sbuf_tensor("sln1", [128, NTA * D], BF16))
            sln1f = EA(nc.sbuf_tensor("sln1f", [128, 8 * CHA], BF16))
            sq = EA(nc.sbuf_tensor("sq", [128, 8 * CHA], BF16))
            sk = EA(nc.sbuf_tensor("sk", [128, 8 * (CHA + 128)], BF16))
            sv_ = EA(nc.sbuf_tensor("sv", [128, (NTA + 1) * D], BF16))
            srq = EA(nc.sbuf_tensor("srq", [128, 2 * CHA], BF16))
            sbn = EA(nc.sbuf_tensor("sbn", [128, NTA * 12], F32))
            sbm = EA(nc.sbuf_tensor("sbm", [128, NTA * 2], F32))
            srstd = EA(nc.sbuf_tensor("srstd", [128, NTA], F32))
            sp = EA(nc.sbuf_tensor("sp", [128, 4 * 1024], BF16))
            spT = EA(nc.sbuf_tensor("spT", [128, 2 * 1024], BF16))
            snmax = EA(nc.sbuf_tensor("snmax", [128, 16], F32))
            ssum = EA(nc.sbuf_tensor("ssum", [128, 16], F32))
            srcl = EA(nc.sbuf_tensor("srcl", [128, 16], F32))
            sloc = EA(nc.sbuf_tensor("sloc", [128, 8 * CHA], BF16))
            spoolT = EA(nc.sbuf_tensor("spoolT", [128, 8 * NLC], BF16))
            srk = EA(nc.sbuf_tensor("srk", [128, 2 * NLAT], BF16))
            srvf = EA(nc.sbuf_tensor("srvf", [128, 2 * NLAT], BF16))
            srv = EA(nc.sbuf_tensor("srv", [64, LATD], BF16))
            slp = EA(nc.sbuf_tensor("slp", [128, LHN * NLAT], BF16))
            slpT = EA(nc.sbuf_tensor("slpT", [64, 2 * LHN * 128], BF16))
            slnmax = EA(nc.sbuf_tensor("slnmax", [128, LHN], F32))
            slsum = EA(nc.sbuf_tensor("slsum", [128, LHN], F32))
            slrcl = EA(nc.sbuf_tensor("slrcl", [128, LHN], F32))
            slat = EA(nc.sbuf_tensor("slat", [128, 2 * CHA], BF16))
            srem = EA(nc.sbuf_tensor("srem", [128, 8 * CHA], BF16))
            sg1 = EA(nc.sbuf_tensor("sg1", [16, CHA], F32))
            sgbf = EA(nc.sbuf_tensor("sgbf", [16, CHA], BF16))
            sdiff = EA(nc.sbuf_tensor("sdiff", [128, 512], BF16))
            smx1 = EA(nc.sbuf_tensor("smx1", [128, 512], BF16))

            for dc in range(8):
                ld_cast(wqkv[:, dc * 3 * D:(dc + 1) * 3 * D],
                        din["wqkv"][dc * 128:(dc + 1) * 128, :], sWA)
            for dc in range(8):
                ld_cast(wrq[:, dc * LATD:(dc + 1) * LATD],
                        din["wrq"][dc * 128:(dc + 1) * 128, :], sWA)
                ld_cast(wrkv[:, dc * 2 * LATD:(dc + 1) * 2 * LATD],
                        din["wrkv"][dc * 128:(dc + 1) * 128, :], sWA)
                ld_cast(wout[:, dc * D:(dc + 1) * D],
                        din["wout"][dc * 128:(dc + 1) * 128, :], sWA)
            for dc in range(2):
                ld_cast(wrout[:, dc * D:(dc + 1) * D],
                        din["wrout"][dc * 128:(dc + 1) * 128, :], sWA)
            g_wa_loads = sWA.n

            def x_load(b, ch, waitv):
                par = ch % 2
                src = x_d[b * S + ch * CHA: b * S + (ch + 1) * CHA, :]
                if waitv is not None:
                    P.W("sync", waitv[0], waitv[1])
                P.E("sync", "dma_start",
                    out=sx[par][:].rearrange("p (t d) -> p t d", d=D),
                    in_=src.rearrange("(t p) d -> p t d", p=128),
                    inc=(sX[par], 16))
                return (sX[par], sX[par].bump(16))

            xload_done = {}
            xbuf_free = {0: None, 1: None}
            xload_done[(0, 0)] = x_load(0, 0, None)

            rot = {"slot": 0, "free": [None, None]}

            def proj_group(mms, evac, extra=(), per_mm_waits=None):
                slot = rot["slot"]
                rot["slot"] ^= 1
                bank = pA if slot == 0 else pB
                w = [x for x in extra if x is not None]
                if rot["free"][slot] is not None:
                    w.append(rot["free"][slot])
                n = len(mms)
                for i, (lh, rh) in enumerate(mms):
                    ww = list(w if i == 0 else ())
                    if per_mm_waits is not None:
                        ww += per_mm_waits[i]
                    P.E("tensor", "matmul", bank[:, 0:rh.shape[-1]], lh, rh,
                        start=(i == 0), stop=(i == n - 1),
                        waits=ww,
                        inc=((sT, 1) if i == n - 1 else None))
                tv = sT.bump()
                ev = evac(bank, tv)
                rot["free"][slot] = ev
                return ev

            def sc_evac(dst_ap, bias_ap, scale, width=512):
                def ev(bank, tv):
                    P.E("scalar", "activation", dst_ap, bank[:, 0:width],
                        ACTF.Identity, bias=bias_ap, scale=scale,
                        waits=[(sT, tv)], inc=(sS, 1))
                    return (sS, sS.bump())
                return ev

            def v_evac(dst_ap, width=512):
                def ev(bank, tv):
                    P.E("vector", "tensor_copy", dst_ap, bank[:, 0:width],
                        waits=[(sT, tv)], inc=(sV, 1))
                    return (sV, sV.bump())
                return ev

            pt1_free = [None]
            s_exp_last = None
            halo_prev = None
            ln1_done = {}

            def emit_ln1(cx, xv, guard=None):
                P.W("vector", xv[0], xv[1])
                if guard is not None:
                    P.W("vector", guard[0], guard[1])
                for ti in range(NTA):
                    P.E("vector", "bn_stats", sbn[:, ti * 12:ti * 12 + 6],
                        cx[:, ti * D:ti * D + 512])
                    P.E("vector", "bn_stats",
                        sbn[:, ti * 12 + 6:ti * 12 + 12],
                        cx[:, ti * D + 512:(ti + 1) * D])
                P.D("vector")
                for ti in range(NTA):
                    P.E("vector", "bn_aggr", sbm[:, ti * 2:ti * 2 + 2],
                        sbn[:, ti * 12:ti * 12 + 12])
                P.D("vector")
                P.E("vector", "engine_nop", inc=(sV, 1))
                v_stats = sV.bump()
                var_ap = sbm[:].rearrange("p (t c) -> p t c", c=2)[:, :, 1]
                P.E("scalar", "activation", srstd[:], var_ap, ACTF.Ln,
                    bias=seps[:, 0:1], scale=1.0, waits=[(sV, v_stats)])
                P.D("scalar")
                P.E("scalar", "activation", srstd[:], srstd[:], ACTF.Exp,
                    bias=0.0, scale=-0.5, inc=(sS, 1))
                s_rstd = sS.bump()
                mean_ap = sbm[:].rearrange("p (t c) -> p t c", c=2)[:, :, 0]
                P.W("vector", sS, s_rstd)
                for ti in range(NTA):
                    P.E("vector", "tensor_scalar",
                        sln1[:, ti * D:(ti + 1) * D],
                        cx[:, ti * D:(ti + 1) * D],
                        mean_ap[:, ti:ti + 1], srstd[:, ti:ti + 1],
                        ALU.subtract, ALU.mult)
                P.D("vector")
                P.E("vector", "engine_nop", inc=(sV, 1))
                return sV.bump()

            for b in range(NSEQ):
                P.W("vector", sWA, g_wa_loads)
                P.W("vector", sV, v_const_ready)
                P.E("vector", "memset", srk[:], 0.0)
                P.E("vector", "memset", srv[:], 0.0)
                P.E("vector", "memset", srvf[:], 0.0)
                P.E("vector", "memset",
                    sk[:].rearrange("p (c k) -> p c k", c=8)[:, :, 0:128], 0.0)
                P.E("vector", "memset", sv_[:, 0:D], 0.0)
                P.D("vector")
                P.E("vector", "engine_nop", inc=(sV, 1))
                seq_reset = sV.bump()

                for ch in range(NCH_A):
                    xv = xload_done[(b, ch)]
                    nb, nch = (b, ch + 1) if ch + 1 < NCH_A else (b + 1, 0)
                    if nb < NSEQ:
                        xload_done[(nb, nch)] = x_load(
                            nb, nch, xbuf_free[(ch + 1) % 2])
                    cx = sx[ch % 2]

                    if (b, ch) in ln1_done:
                        v_ln1 = ln1_done[(b, ch)]
                    else:
                        v_ln1 = emit_ln1(cx, xv)

                    if LEVEL < 2:
                        continue
                    l0 = ch * NLC
                    w0 = [(sV, v_ln1), (sS, sS.n), halo_prev,
                          rot["free"][0], rot["free"][1]]
                    if ch == 0 and b == 0:
                        w0 += [(sWA, g_wa_loads), (sLX, v_init_loads)]
                    for dc in range(8):
                        for ti in range(NTA):
                            last = (dc == 7 and ti == NTA - 1)
                            P.E("tensor", "matmul",
                                pG[:, dc * NLC + ti * LPT:
                                   dc * NLC + (ti + 1) * LPT],
                                sln1[:, ti * D + dc * 128:
                                     ti * D + (dc + 1) * 128],
                                indp[:],
                                start=True, stop=True,
                                waits=(w0 if (dc == 0 and ti == 0) else ()),
                                inc=((sT, 1) if last else None))
                    t_pool = sT.bump()
                    P.E("vector", "tensor_copy", spoolT[:, 0:8 * NLC],
                        pG[:, 0:8 * NLC], waits=[(sT, t_pool)],
                        inc=(sV, 1))
                    v_poolT = sV.bump()

                    for jc in range(4):
                        mms = [(wrkv[:, dc * 2 * LATD + jc * 128:
                                     dc * 2 * LATD + (jc + 1) * 128],
                                spoolT[:, dc * NLC:(dc + 1) * NLC])
                               for dc in range(8)]
                        if jc < 2:
                            dst = srk[:, jc * NLAT + l0:jc * NLAT + l0 + NLC]
                        else:
                            dst = srvf[:, (jc - 2) * NLAT + l0:
                                       (jc - 2) * NLAT + l0 + NLC]
                        proj_group(
                            mms, sc_evac(dst, rkvb[:, jc:jc + 1], 1.0,
                                         width=NLC),
                            extra=([(sV, v_poolT)] if jc == 0 else ()))
                    rkv_done = sS.n
                    for pr in range(2):
                        P.E("tensor", "transpose",
                            pT1[0:NLAT, pr * 128:(pr + 1) * 128],
                            srvf[:, pr * NLAT:(pr + 1) * NLAT],
                            ident[:],
                            waits=([(sS, rkv_done), pt1_free[0]]
                                   if pr == 0 else ()),
                            inc=((sT, 1) if pr == 1 else None))
                    t_rvT = sT.bump()
                    P.E("vector", "tensor_copy", srv[0:NLAT, 0:256],
                        pT1[0:NLAT, 0:256], waits=[(sT, t_rvT)],
                        inc=(sV, 1))
                    v_lat_w = sV.bump()
                    pt1_free[0] = (sV, v_lat_w)

                    if LEVEL < 3:
                        continue
                    tT_prev = v_lat_w
                    for ti in range(NTA):
                        for dc in range(8):
                            P.E("tensor", "transpose",
                                pT1[:, dc * 128:(dc + 1) * 128],
                                sln1[:, ti * D + dc * 128:
                                     ti * D + (dc + 1) * 128],
                                ident[:],
                                waits=(([(sV, tT_prev), pt1_free[0]])
                                       if dc == 0 else ()),
                                inc=((sT, 1) if dc == 7 else None))
                        tv = sT.bump()
                        dst = bass.AP(sln1f, ti * 128,
                                      [[8 * CHA, 128], [CHA, 8], [1, 128]])
                        P.E("vector", "tensor_copy", dst,
                            pT1[:].rearrange("p (c k) -> p c k", c=8),
                            waits=[(sT, tv)], inc=(sV, 1))
                        tT_prev = sV.bump()
                        pt1_free[0] = (sV, tT_prev)
                    v_ln1f = tT_prev
                    t_a3 = sT.n

                    if LEVEL < 4:
                        continue
                    extra0 = [(sV, v_ln1f)]
                    for jc in range(8):
                        mms = [(wqkv[:, dc * 3 * D + jc * 128:
                                     dc * 3 * D + (jc + 1) * 128],
                                sln1f[:, dc * CHA:dc * CHA + 512])
                               for dc in range(8)]
                        proj_group(mms,
                                   sc_evac(sq[:, jc * CHA:jc * CHA + 512],
                                           qkvb[:, jc:jc + 1], 0.125),
                                   extra=extra0)
                        extra0 = ()
                    for jc in range(8):
                        mms = [(wqkv[:, dc * 3 * D + D + jc * 128:
                                     dc * 3 * D + D + (jc + 1) * 128],
                                sln1f[:, dc * CHA:dc * CHA + 512])
                               for dc in range(8)]
                        proj_group(mms,
                                   sc_evac(sk[:, jc * (CHA + 128) + 128:
                                              jc * (CHA + 128) + 640],
                                           qkvb[:, 8 + jc:9 + jc], 1.0))
                    for ti in range(NTA):
                        for jh in range(2):
                            mms = [(sln1f[:, dc * CHA + ti * 128:
                                          dc * CHA + (ti + 1) * 128],
                                    wqkv[:, dc * 3 * D + 2 * D + jh * 512:
                                         dc * 3 * D + 2 * D + (jh + 1) * 512])
                                   for dc in range(8)]
                            mms.append((onesrow[0:1, 0:128],
                                        vbrow[0:1, jh * 512:(jh + 1) * 512]))
                            proj_group(mms, v_evac(
                                sv_[:, (1 + ti) * D + jh * 512:
                                    (1 + ti) * D + (jh + 1) * 512]))
                    for jc in range(2):
                        mms = [(wrq[:, dc * LATD + jc * 128:
                                    dc * LATD + (jc + 1) * 128],
                                sln1f[:, dc * CHA:dc * CHA + 512])
                               for dc in range(8)]
                        proj_group(mms,
                                   sc_evac(srq[:, jc * CHA:jc * CHA + 512],
                                           rqb[:, jc:jc + 1], 0.125))
                    qkv_done_s = sS.n
                    qkv_done_v = sV.n
                    if nb < NSEQ:
                        ln1_done[(nb, nch)] = emit_ln1(
                            sx[nch % 2], xload_done[(nb, nch)],
                            guard=(sT, t_a3))

                    if LEVEL < 5:
                        continue
                    st5 = {
                        "slotfree": {0: s_exp_last, 1: s_exp_last},
                        "sp_free": {0: None, 1: None, 2: None, 3: None},
                        "loc_free": rot["free"][0],
                        "first5": [(sV, qkv_done_v)],
                        "ctx": {},
                        "lat_prev": None,
                    }

                    def a5_front(idx, ti, g):
                        slot = g % 2
                        sp4 = idx % 4
                        bankS = pC if slot == 0 else pD
                        w = [(sS, qkv_done_s)] + st5["first5"]
                        st5["first5"] = []
                        if st5["slotfree"][slot] is not None:
                            w.append(st5["slotfree"][slot])
                        for i_, s_ in enumerate((0, 2, 1, 3)):
                            h = g * 4 + s_
                            jc = h // 2
                            r0 = 64 * (h % 2)
                            c0 = (h % 2) * 512 + ((h // 2) % 2) * 256
                            P.E("tensor", "matmul",
                                bankS[:, c0:c0 + 256],
                                sq[r0:r0 + 64, jc * CHA + ti * 128:
                                   jc * CHA + (ti + 1) * 128],
                                sk[r0:r0 + 64, jc * (CHA + 128) + ti * 128:
                                   jc * (CHA + 128) + ti * 128 + 256],
                                start=True, stop=True,
                                waits=(w if i_ == 0 else ()),
                                inc=((sT, 1) if i_ == 3 else None))
                        t_sc = sT.bump()
                        P.E("vector", "reduce_max",
                            snmax[:, g * 4:(g + 1) * 4],
                            bankS[:].rearrange("p (s k) -> p s k", s=4),
                            axis=AXX, negate=True,
                            waits=[(sT, t_sc)])
                        P.D("vector")
                        P.E("vector", "engine_nop", inc=(sV, 1))
                        v_msk = sV.bump()
                        wsp = [(sV, v_msk), st5["sp_free"][sp4]]
                        for s_ in range(4):
                            c0 = (s_ % 2) * 512 + ((s_ // 2) % 2) * 256
                            j_ = (s_ % 2) * 2 + s_ // 2
                            P.E("scalar", "activation",
                                sp[:, sp4 * 1024 + s_ * 256:
                                   sp4 * 1024 + (s_ + 1) * 256],
                                bankS[:, c0:c0 + 256], ACTF.Exp,
                                bias=snmax[:, g * 4 + j_:g * 4 + j_ + 1],
                                scale=1.0,
                                waits=(wsp if s_ == 0 else ()),
                                inc=((sS, 1) if s_ == 3 else None))
                        s_exp = sS.bump()
                        st5["slotfree"][slot] = (sS, s_exp)
                        st5["ctx"][idx] = s_exp

                    def a5_back1(idx, ti, g):
                        sp4 = idx % 4
                        s_exp = st5["ctx"][idx]
                        mrow = 1 if (ch == 0 and ti == 0) else 0
                        for s_ in range(4):
                            P.E("vector", "scalar_tensor_tensor",
                                sp[:, sp4 * 1024 + s_ * 256:
                                   sp4 * 1024 + (s_ + 1) * 256],
                                sp[:, sp4 * 1024 + s_ * 256:
                                   sp4 * 1024 + (s_ + 1) * 256],
                                1.0,
                                bass.AP(mloc, mrow * 256,
                                        [[512, 128], [1, 256]]),
                                ALU.mult, ALU.mult,
                                accum_out=ssum[:, g * 4 + s_:g * 4 + s_ + 1],
                                waits=([(sS, s_exp)] if s_ == 0 else ()),
                                inc=((sV, 1) if s_ == 3 else None))
                        v_pm = sV.bump()
                        P.E("scalar", "activation",
                            srcl[:, g * 4:(g + 1) * 4],
                            ssum[:, g * 4:(g + 1) * 4], ACTF.Ln,
                            bias=seps30[:, 0:1], scale=1.0,
                            waits=[(sV, v_pm)])
                        P.D("scalar")
                        P.E("scalar", "activation",
                            srcl[:, g * 4:(g + 1) * 4],
                            srcl[:, g * 4:(g + 1) * 4], ACTF.Exp,
                            bias=0.0, scale=-1.0, inc=(sS, 1))
                        s_rc = sS.bump()
                        P.W("vector", sS, s_rc)
                        for s_ in range(4):
                            P.E("vector", "tensor_scalar_mul",
                                sp[:, sp4 * 1024 + s_ * 256:
                                   sp4 * 1024 + (s_ + 1) * 256],
                                sp[:, sp4 * 1024 + s_ * 256:
                                   sp4 * 1024 + (s_ + 1) * 256],
                                srcl[:, g * 4 + s_:g * 4 + s_ + 1],
                                inc=((sV, 1) if s_ == 3 else None))
                        st5["ctx"][(idx, "norm")] = sV.bump()

                    def a5_back2a(idx, ti, g):
                        slot = g % 2
                        sp4 = idx % 4
                        g_p = st5["ctx"][(idx, "norm")]
                        wtr = [(sV, g_p), pt1_free[0]]
                        for s_ in range(4):
                            for kc in range(2):
                                P.E("tensor", "transpose",
                                    pT1[:, s_ * 256 + kc * 128:
                                        s_ * 256 + (kc + 1) * 128],
                                    sp[:, sp4 * 1024 + s_ * 256 + kc * 128:
                                       sp4 * 1024 + s_ * 256 +
                                       (kc + 1) * 128],
                                    ident[:],
                                    waits=(wtr if (s_ == 0 and kc == 0)
                                           else ()),
                                    inc=((sT, 1) if (s_ == 3 and kc == 1)
                                         else None))
                        t_ptr = sT.bump()
                        st5["sp_free"][sp4] = (sT, t_ptr)
                        P.E("scalar", "copy",
                            spT[:, slot * 1024:(slot + 1) * 1024],
                            pT1[:, 0:1024], waits=[(sT, t_ptr)],
                            inc=(sS, 1))
                        s_pT = sS.bump()
                        pt1_free[0] = (sS, s_pT)
                        st5["ctx"][(idx, "pT")] = s_pT

                    def a5_back2b(idx, ti, g):
                        slot = g % 2
                        s_pT = st5["ctx"][(idx, "pT")]
                        wpv = [(sS, s_pT), st5["loc_free"]]
                        for s_ in range(4):
                            h = g * 4 + s_
                            pr = s_ // 2
                            r0 = 64 * (h % 2)
                            for kc in range(2):
                                P.E("tensor", "matmul",
                                    pA[r0:r0 + 64, pr * 128:(pr + 1) * 128],
                                    sv_[:, (ti + kc) * D + h * 64:
                                        (ti + kc) * D + (h + 1) * 64],
                                    spT[:, slot * 1024 + s_ * 256 + kc * 128:
                                        slot * 1024 + s_ * 256 +
                                        (kc + 1) * 128],
                                    start=(kc == 0), stop=(kc == 1),
                                    waits=(wpv if (s_ == 0 and kc == 0)
                                           else ()),
                                    inc=((sT, 1) if (s_ == 3 and kc == 1)
                                         else None))
                        t_pv = sT.bump()
                        dstl = bass.AP(sloc, (2 * g) * CHA + ti * 128,
                                       [[8 * CHA, 128], [CHA, 2], [1, 128]])
                        P.E("scalar", "copy", dstl,
                            pA[:, 0:256].rearrange("p (a k) -> p a k", a=2),
                            waits=[(sT, t_pv)], inc=(sS, 1))
                        st5["loc_free"] = (sS, sS.bump())
                        rot["free"][0] = st5["loc_free"]

                    def a6a(ti):
                        tg = ch * NTA + ti
                        wl = [(sS, qkv_done_s), (sV, v_lat_w),
                              st5["lat_prev"],
                              rot["free"][1] if ti == 0 else None]
                        for lh in range(4):
                            r0 = 64 * (lh % 2)
                            jc = lh // 2
                            pb_ = pG if lh % 2 == 0 else pB
                            P.E("tensor", "matmul",
                                pb_[:, (lh // 2) * 64:(lh // 2 + 1) * 64],
                                srq[r0:r0 + 64, jc * CHA + ti * 128:
                                    jc * CHA + (ti + 1) * 128],
                                srk[r0:r0 + 64, jc * NLAT:(jc + 1) * NLAT],
                                start=True, stop=True,
                                waits=(wl if lh == 0 else ()),
                                inc=((sT, 1) if lh == 3 else None))
                        t_lsc = sT.bump()
                        for pi, pb_ in enumerate((pG, pB)):
                            P.E("vector", "reduce_max",
                                slnmax[:, pi * 2:(pi + 1) * 2],
                                pb_[:, 0:128].rearrange(
                                    "p (s k) -> p s k", s=2),
                                axis=AXX, negate=True,
                                waits=([(sT, t_lsc)] if pi == 0 else ()))
                        P.D("vector")
                        P.E("vector", "engine_nop", inc=(sV, 1))
                        v_lm = sV.bump()
                        for lh in range(4):
                            pb_ = pG if lh % 2 == 0 else pB
                            j_ = (lh % 2) * 2 + lh // 2
                            P.E("scalar", "activation",
                                slp[:, lh * NLAT:(lh + 1) * NLAT],
                                pb_[:, (lh // 2) * 64:(lh // 2 + 1) * 64],
                                ACTF.Exp,
                                bias=slnmax[:, j_:j_ + 1], scale=1.0,
                                waits=([(sV, v_lm)] if lh == 0 else ()),
                                inc=((sS, 1) if lh == 3 else None))
                        s_lexp = sS.bump()
                        rot["free"][1] = (sS, s_lexp)
                        for lh in range(4):
                            P.E("vector", "scalar_tensor_tensor",
                                slp[:, lh * NLAT:(lh + 1) * NLAT],
                                slp[:, lh * NLAT:(lh + 1) * NLAT],
                                1.0,
                                bass.AP(mlat, 64 - LPT * tg,
                                        [[128, 128], [1, NLAT]]),
                                ALU.mult, ALU.mult,
                                accum_out=slsum[:, lh:lh + 1],
                                waits=([(sS, s_lexp)] if lh == 0 else ()),
                                inc=((sV, 1) if lh == 3 else None))
                        v_lpm = sV.bump()
                        P.E("scalar", "activation", slrcl[:], slsum[:],
                            ACTF.Ln, bias=seps30[:, 0:1], scale=1.0,
                            waits=[(sV, v_lpm)])
                        P.D("scalar")
                        P.E("scalar", "activation", slrcl[:], slrcl[:],
                            ACTF.Exp, bias=0.0, scale=-1.0, inc=(sS, 1))
                        s_lrc = sS.bump()
                        P.W("vector", sS, s_lrc)
                        for lh in range(4):
                            P.E("vector", "tensor_scalar_mul",
                                slp[:, lh * NLAT:(lh + 1) * NLAT],
                                slp[:, lh * NLAT:(lh + 1) * NLAT],
                                slrcl[:, lh:lh + 1],
                                inc=((sV, 1) if lh == 3 else None))
                        g_lp = sV.bump()
                        wlt = [(sV, g_lp), pt1_free[0]]
                        for lh in range(4):
                            P.E("tensor", "transpose",
                                pT1[0:NLAT, lh * 128:(lh + 1) * 128],
                                slp[:, lh * NLAT:(lh + 1) * NLAT],
                                ident[:],
                                waits=(wlt if lh == 0 else ()),
                                inc=((sT, 1) if lh == 3 else None))
                        t_lT = sT.bump()
                        P.E("vector", "tensor_copy",
                            slpT[:, (ti % 2) * 512:(ti % 2 + 1) * 512],
                            pT1[0:NLAT, 0:512], waits=[(sT, t_lT)],
                            inc=(sV, 1))
                        v_lT = sV.bump()
                        pt1_free[0] = (sV, v_lT)
                        st5["ctx"][(ti, "lT")] = v_lT

                    def a6b(ti):
                        v_lT = st5["ctx"][(ti, "lT")]
                        for lh in range(4):
                            pr = lh // 2
                            r0 = 64 * (lh % 2)
                            P.E("tensor", "matmul",
                                pG[r0:r0 + 64,
                                   256 + pr * 128:256 + (pr + 1) * 128],
                                srv[0:NLAT, lh * 64:(lh + 1) * 64],
                                slpT[0:NLAT, (ti % 2) * 512 + lh * 128:
                                     (ti % 2) * 512 + (lh + 1) * 128],
                                start=True, stop=True,
                                waits=([(sV, v_lT)] if lh == 0 else ()),
                                inc=((sT, 1) if lh == 3 else None))
                        t_lpv = sT.bump()
                        dstL = bass.AP(slat, ti * 128,
                                       [[2 * CHA, 128], [CHA, 2], [1, 128]])
                        P.E("vector", "tensor_copy", dstL,
                            pG[:, 256:512].rearrange("p (a k) -> p a k", a=2),
                            waits=[(sT, t_lpv)], inc=(sV, 1))
                        st5["lat_prev"] = (sV, sV.bump())

                    items = [(ti_, g_) for ti_ in range(NTA)
                             for g_ in range(4)]
                    n_it = len(items)
                    for idx, (ti_, g_) in enumerate(items):
                        a5_front(idx, ti_, g_)
                        if idx >= 1:
                            a5_back1(idx - 1, *items[idx - 1])
                        if idx >= 2:
                            a5_back2a(idx - 2, *items[idx - 2])
                        if idx >= 3:
                            a5_back2b(idx - 3, *items[idx - 3])
                        if g_ == 3 and LEVEL >= 6:
                            if ti_ >= 1:
                                a6b(ti_ - 1)
                            a6a(ti_)
                    a5_back1(n_it - 1, *items[n_it - 1])
                    a5_back2a(n_it - 2, *items[n_it - 2])
                    a5_back2b(n_it - 3, *items[n_it - 3])
                    a5_back2a(n_it - 1, *items[n_it - 1])
                    a5_back2b(n_it - 2, *items[n_it - 2])
                    a5_back2b(n_it - 1, *items[n_it - 1])
                    if LEVEL >= 6:
                        a6b(NTA - 1)
                    lat_prev = st5["lat_prev"]
                    loc_free = st5["loc_free"]

                    lat_done = lat_prev

                    P.W("vector", sT, sT.n)
                    P.E("vector", "tensor_copy",
                        sk[:].rearrange("p (c k) -> p c k", c=8)[:, :, 0:128],
                        sk[:].rearrange("p (c k) -> p c k",
                                        c=8)[:, :, CHA:CHA + 128])
                    P.E("vector", "tensor_copy", sv_[:, 0:D],
                        sv_[:, NTA * D:(NTA + 1) * D])
                    P.D("vector")
                    P.E("vector", "engine_nop", inc=(sV, 1))
                    halo_prev = (sV, sV.bump())

                    if LEVEL < 7:
                        continue
                    extra0 = [lat_done]
                    for jc in range(8):
                        mms = [(wrout[:, dc * D + jc * 128:
                                      dc * D + (jc + 1) * 128],
                                slat[:, dc * CHA:dc * CHA + 512])
                               for dc in range(2)]
                        proj_group(mms,
                                   sc_evac(srem[:, jc * CHA:jc * CHA + 512],
                                           routb[:, jc:jc + 1], 1.0),
                                   extra=extra0)
                        extra0 = ()
                    rem_done_s = sS.n

                    wg = [(sS, rem_done_s), lat_done]
                    for dc in range(8):
                        P.E("tensor", "matmul", pG[0:16, 0:512],
                            gstk[:, dc * 16:(dc + 1) * 16],
                            sloc[:, dc * CHA:dc * CHA + 512],
                            start=(dc == 0), stop=False,
                            waits=(wg if dc == 0 else ()))
                    for dc in range(8):
                        P.E("tensor", "matmul", pG[0:16, 0:512],
                            gstk[:, 128 + dc * 16:128 + (dc + 1) * 16],
                            srem[:, dc * CHA:dc * CHA + 512],
                            start=False, stop=(dc == 7),
                            inc=((sT, 1) if dc == 7 else None))
                    t_gate = sT.bump()
                    P.E("scalar", "activation", sg1[0:16, :],
                        pG[0:16, 0:512], ACTF.Exp, bias=0.0, scale=-1.0,
                        waits=[(sT, t_gate)], inc=(sS, 1))
                    s_g1 = sS.bump()
                    P.D("scalar")
                    P.E("scalar", "activation", sg1[0:16, :], sg1[0:16, :],
                        ACTF.Ln, bias=sone1[0:16, 0:1], scale=1.0)
                    P.D("scalar")
                    P.E("scalar", "activation", sgbf[0:16, :], sg1[0:16, :],
                        ACTF.Exp, bias=0.0, scale=-1.0, inc=(sS, 1))
                    s_gate = sS.bump()
                    mix_prev = None
                    mixv = {}
                    for dc in range(8):
                        w = [(sS, s_gate), mix_prev]
                        P.E("tensor", "matmul", pG[:, 0:512],
                            estk[0:16, dc * 128:(dc + 1) * 128],
                            sgbf[0:16, 0:512], start=True, stop=True,
                            waits=w, inc=(sT, 1))
                        t_ge = sT.bump()
                        P.E("vector", "tensor_tensor", sdiff[:],
                            sloc[:, dc * CHA:dc * CHA + 512],
                            srem[:, dc * CHA:dc * CHA + 512],
                            ALU.subtract, waits=[(sT, t_ge)])
                        P.D("vector")
                        P.E("vector", "tensor_tensor", smx1[:],
                            pG[:, 0:512], sdiff[:], ALU.mult)
                        P.D("vector")
                        P.E("vector", "tensor_tensor",
                            sln1f[:, dc * CHA:dc * CHA + 512],
                            smx1[:], srem[:, dc * CHA:dc * CHA + 512],
                            ALU.add, inc=(sV, 1))
                        mix_prev = (sV, sV.bump())
                        mixv[dc] = mix_prev[1]
                    v_mixed = mix_prev[1]

                    if LEVEL < 8:
                        continue
                    ev = None
                    for ti in range(NTA):
                        for jh in range(2):
                            first9 = (ti == 0 and jh == 0)
                            mms = [(sln1f[:, dc * CHA + ti * 128:
                                          dc * CHA + (ti + 1) * 128],
                                    wout[:, dc * D + jh * 512:
                                         dc * D + (jh + 1) * 512])
                                   for dc in range(8)]
                            mms.append((onesrow[0:1, 0:128],
                                        outbrow[0:1,
                                                jh * 512:(jh + 1) * 512]))

                            def mk_res_evac(ti=ti, jh=jh):
                                def evf(bank, tv):
                                    c0 = ti * D + jh * 512
                                    P.E("vector", "tensor_tensor",
                                        cx[:, c0:c0 + 512],
                                        bank[:, 0:512], cx[:, c0:c0 + 512],
                                        ALU.add, waits=[(sT, tv)],
                                        inc=(sV, 1))
                                    return (sV, sV.bump())
                                return evf

                            if first9:
                                ev = proj_group(
                                    mms, mk_res_evac(),
                                    extra=[(sV, mixv[dc2]) for dc2 in
                                           range(8)][:1],
                                    per_mm_waits=[[(sV, mixv[dc2])]
                                                  for dc2 in range(8)] +
                                                 [[]])
                            else:
                                ev = proj_group(mms, mk_res_evac())
                    v_x2 = ev[1]
                    P.W("sync", sV, v_x2)
                    dst = x2_d[b * S + ch * CHA: b * S + (ch + 1) * CHA, :]
                    P.E("sync", "dma_start",
                        out=dst.rearrange("(t p) d -> p t d", p=128),
                        in_=cx[:].rearrange("p (t d) -> p t d", d=D),
                        inc=(sSX[ch % 2], 16))
                    xbuf_free[ch % 2] = (sSX[ch % 2], sSX[ch % 2].bump(16))

            a_stores = (sSX[0].n, sSX[1].n)
            P.flush()

        with ExitStack() as esB:
            EB = esB.enter_context
            sw1 = [EB(nc.sbuf_tensor(f"sw1_{i}", [128, 8 * HFF], BF16))
                   for i in range(3)]
            sw2 = [EB(nc.sbuf_tensor(f"sw2_{i}", [128, 8 * D], BF16))
                   for i in range(3)]
            sxb = [EB(nc.sbuf_tensor(f"sxb{i}", [128, NTB * D], F32))
                   for i in range(2)]
            sln2 = EB(nc.sbuf_tensor("sln2", [128, NTB * D], BF16))
            sln2f = EB(nc.sbuf_tensor("sln2f", [128, 2 * 8 * CHB], BF16))
            sbn2 = EB(nc.sbuf_tensor("sbn2", [128, NTB * 12], F32))
            sbm2 = EB(nc.sbuf_tensor("sbm2", [128, NTB * 2], F32))
            srstd2 = EB(nc.sbuf_tensor("srstd2", [128, NTB], F32))
            shid = [EB(nc.sbuf_tensor(f"shid{i}", [128, 8 * 512], BF16))
                    for i in range(2)]
            sacc = EB(nc.sbuf_tensor("sacc", [128, NTB * D], F32))
            slg = EB(nc.sbuf_tensor("slg", [128, 2 * (NTB * EE)], F32))
            sz = EB(nc.sbuf_tensor("sz", [128, 2 * (NTB * EE)], F32))
            sm12 = EB(nc.sbuf_tensor("sm12", [128, 2 * 2 * NTB], F32))
            sge = EB(nc.sbuf_tensor("sge", [128, 2 * (NTB * EE)], F32))
            swsel = EB(nc.sbuf_tensor("swsel", [128, 2 * (NTB * EE)], F32))
            swe = EB(nc.sbuf_tensor("swe", [128, 2 * (NTB * EE)], F32))
            ssum2 = EB(nc.sbuf_tensor("ssum2", [128, 2 * (NTB)], F32))
            srelu = EB(nc.sbuf_tensor("srelu", [128, 2 * 512], BF16))

            def wload(e, slot, after=(), after2=None):
                if e < EE:
                    src1, src2 = din["w1t"][e], din["w2t"][e]
                else:
                    src1, src2 = din["cfct"], din["cprjt"]
                for wv_ in after:
                    if wv_ is not None:
                        P.W("gpsimd", wv_[0], wv_[1])
                sm1, sm2 = sW[slot], sW2s[slot]
                P.E("gpsimd", "dma_start",
                    out=sw1[slot][:].rearrange("p (c k) -> p c k", c=8),
                    in_=src1.rearrange("(c p) k -> p c k", p=128),
                    inc=(sm1, 16))
                v1 = sm1.bump(16)
                if after2 is not None:
                    P.W("gpsimd", after2[0], after2[1])
                P.E("gpsimd", "dma_start",
                    out=sw2[slot][:].rearrange("p (c k) -> p c k", c=8),
                    in_=src2.rearrange("(c p) k -> p c k", p=128),
                    inc=(sm2, 16))
                return ((sm1, v1), (sm2, sm2.bump(16)))

            rotB = {"slot": 0, "free": [None, None]}

            def proj_groupB(pool, mms, evac, extra=()):
                slot = rotB["slot"]
                rotB["slot"] ^= 1
                bank = pool[slot]
                w = [x for x in extra if x is not None]
                if rotB["free"][slot] is not None:
                    w.append(rotB["free"][slot])
                n = len(mms)
                for i, (lh, rh) in enumerate(mms):
                    P.E("tensor", "matmul", bank[:, 0:rh.shape[-1]], lh, rh,
                        start=(i == 0), stop=(i == n - 1),
                        waits=(w if i == 0 else ()),
                        inc=((sT, 1) if i == n - 1 else None))
                tv = sT.bump()
                ev = evac(bank, tv)
                rotB["free"][slot] = ev
                return ev

            xfree = {}
            out_stored = {}
            prep = {}
            tln2f = {}

            def prepB(ch):
                par = ch % 2
                cx = sxb[par]
                if ch == 0:
                    P.W("sync", sSX[0], a_stores[0])
                    P.W("sync", sSX[1], a_stores[1])
                if ch >= 2:
                    P.W("sync", sV, xfree[ch - 2])
                src = x2_d[ch * CHB:(ch + 1) * CHB, :]
                P.E("sync", "dma_start",
                    out=cx[:].rearrange("p (t d) -> p t d", d=D),
                    in_=src.rearrange("(t p) d -> p t d", p=128),
                    inc=(sX[par], 16))
                xv = (sX[par], sX[par].bump(16))

                P.W("vector", xv[0], xv[1])
                for ti in range(NTB):
                    P.E("vector", "bn_stats", sbn2[:, ti * 12:ti * 12 + 6],
                        cx[:, ti * D:ti * D + 512])
                    P.E("vector", "bn_stats",
                        sbn2[:, ti * 12 + 6:ti * 12 + 12],
                        cx[:, ti * D + 512:(ti + 1) * D])
                P.D("vector")
                for ti in range(NTB):
                    P.E("vector", "bn_aggr", sbm2[:, ti * 2:ti * 2 + 2],
                        sbn2[:, ti * 12:ti * 12 + 12])
                P.D("vector")
                P.E("vector", "engine_nop", inc=(sV, 1))
                v_stats = sV.bump()
                var_ap = sbm2[:].rearrange("p (t c) -> p t c", c=2)[:, :, 1]
                P.E("scalar", "activation", srstd2[:], var_ap, ACTF.Ln,
                    bias=seps[:, 0:1], scale=1.0, waits=[(sV, v_stats)])
                P.D("scalar")
                P.E("scalar", "activation", srstd2[:], srstd2[:], ACTF.Exp,
                    bias=0.0, scale=-0.5, inc=(sS, 1))
                s_rstd = sS.bump()
                mean_ap = sbm2[:].rearrange("p (t c) -> p t c", c=2)[:, :, 0]
                P.W("vector", sS, s_rstd)
                if ch >= 1:
                    P.W("vector", sT, tln2f[ch - 1])
                for ti in range(NTB):
                    P.E("vector", "tensor_scalar",
                        sln2[:, ti * D:(ti + 1) * D],
                        cx[:, ti * D:(ti + 1) * D],
                        mean_ap[:, ti:ti + 1], srstd2[:, ti:ti + 1],
                        ALU.subtract, ALU.mult)
                P.D("vector")
                P.E("vector", "engine_nop", inc=(sV, 1))
                v_ln2 = sV.bump()

                f0 = par * 8 * CHB
                tprev = v_ln2
                for ti in range(NTB):
                    for dc in range(8):
                        P.E("tensor", "transpose",
                            pT1[:, dc * 128:(dc + 1) * 128],
                            sln2[:, ti * D + dc * 128:
                                 ti * D + (dc + 1) * 128],
                            ident[:],
                            waits=([(sV, tprev)] if dc == 0 else ()),
                            inc=((sT, 1) if dc == 7 else None))
                    tv = sT.bump()
                    dst = bass.AP(sln2f, f0 + ti * 128,
                                  [[16 * CHB, 128], [CHB, 8], [1, 128]])
                    P.E("vector", "tensor_copy", dst,
                        pT1[:].rearrange("p (c k) -> p c k", c=8),
                        waits=[(sT, tv)], inc=(sV, 1))
                    tprev = sV.bump()
                v_ln2f = tprev
                tln2f[ch] = sT.n

                r0_ = par * NTB * EE
                m0_ = par * 2 * NTB
                for ti in range(NTB):
                    for dc in range(8):
                        P.E("tensor", "matmul",
                            pG[:, ti * EE:(ti + 1) * EE],
                            sln2f[:, f0 + dc * CHB + ti * 128:
                                  f0 + dc * CHB + (ti + 1) * 128],
                            wgt[:, dc * EE:(dc + 1) * EE],
                            start=(dc == 0), stop=(dc == 7),
                            waits=([(sV, v_ln2f)] if (ti == 0 and dc == 0)
                                   else ()),
                            inc=((sT, 1) if (ti == NTB - 1 and dc == 7)
                                 else None))
                t_lg = sT.bump()
                P.E("vector", "tensor_copy", slg[:, r0_:r0_ + NTB * EE],
                    pG[:, 0:NTB * EE], waits=[(sT, t_lg)])
                P.D("vector")
                P.E("vector", "tensor_tensor",
                    sz[:, r0_:r0_ + NTB * EE].rearrange(
                        "p (t e) -> p t e", e=EE),
                    slg[:, r0_:r0_ + NTB * EE].rearrange(
                        "p (t e) -> p t e", e=EE),
                    bass.AP(lb, 0, [[EE, 128], [0, NTB], [1, EE]]),
                    ALU.add)
                P.D("vector")
                P.E("vector", "reduce_max", sm12[:, m0_:m0_ + NTB],
                    sz[:, r0_:r0_ + NTB * EE].rearrange(
                        "p (t e) -> p t e", e=EE), axis=AXX)
                P.D("vector")
                for ti in range(NTB):
                    P.E("vector", "tensor_scalar",
                        sge[:, r0_ + ti * EE:r0_ + (ti + 1) * EE],
                        sz[:, r0_ + ti * EE:r0_ + (ti + 1) * EE],
                        sm12[:, m0_ + ti:m0_ + ti + 1], -1e30,
                        ALU.is_ge, ALU.mult)
                P.D("vector")
                P.E("vector", "tensor_tensor",
                    sge[:, r0_:r0_ + NTB * EE],
                    sz[:, r0_:r0_ + NTB * EE],
                    sge[:, r0_:r0_ + NTB * EE], ALU.add)
                P.D("vector")
                P.E("vector", "reduce_max",
                    sm12[:, m0_ + NTB:m0_ + 2 * NTB],
                    sge[:, r0_:r0_ + NTB * EE].rearrange(
                        "p (t e) -> p t e", e=EE), axis=AXX)
                P.D("vector")
                for ti in range(NTB):
                    P.E("vector", "tensor_scalar",
                        sge[:, r0_ + ti * EE:r0_ + (ti + 1) * EE],
                        sz[:, r0_ + ti * EE:r0_ + (ti + 1) * EE],
                        sm12[:, m0_ + NTB + ti:m0_ + NTB + ti + 1], None,
                        ALU.is_ge)
                P.D("vector")
                P.E("vector", "engine_nop", inc=(sV, 1))
                v_mask = sV.bump()
                P.E("scalar", "activation", swsel[:, r0_:r0_ + NTB * EE],
                    slg[:, r0_:r0_ + NTB * EE], ACTF.Exp,
                    bias=0.0, scale=-1.0, waits=[(sV, v_mask)])
                P.D("scalar")
                P.E("scalar", "activation", swsel[:, r0_:r0_ + NTB * EE],
                    swsel[:, r0_:r0_ + NTB * EE], ACTF.Ln,
                    bias=sone1[:, 0:1], scale=1.0)
                P.D("scalar")
                P.E("scalar", "activation", swsel[:, r0_:r0_ + NTB * EE],
                    swsel[:, r0_:r0_ + NTB * EE], ACTF.Exp,
                    bias=0.0, scale=-1.0, inc=(sS, 1))
                s_sig = sS.bump()
                P.E("vector", "tensor_tensor", swsel[:, r0_:r0_ + NTB * EE],
                    swsel[:, r0_:r0_ + NTB * EE],
                    sge[:, r0_:r0_ + NTB * EE],
                    ALU.mult, waits=[(sS, s_sig)])
                P.D("vector")
                P.E("vector", "reduce_sum", ssum2[:, par * NTB:
                                                  (par + 1) * NTB],
                    swsel[:, r0_:r0_ + NTB * EE].rearrange(
                        "p (t e) -> p t e", e=EE), axis=AXX,
                    inc=(sV, 1))
                v_sum2 = sV.bump()
                P.E("scalar", "activation",
                    ssum2[:, par * NTB:(par + 1) * NTB],
                    ssum2[:, par * NTB:(par + 1) * NTB], ACTF.Ln,
                    bias=seps6[:, 0:1], scale=1.0, waits=[(sV, v_sum2)])
                P.D("scalar")
                P.E("scalar", "activation",
                    ssum2[:, par * NTB:(par + 1) * NTB],
                    ssum2[:, par * NTB:(par + 1) * NTB], ACTF.Exp,
                    bias=0.0, scale=-1.0, inc=(sS, 1))
                s_rc2 = sS.bump()
                P.W("vector", sS, s_rc2)
                for ti in range(NTB):
                    P.E("vector", "tensor_scalar_mul",
                        swe[:, r0_ + ti * EE:r0_ + (ti + 1) * EE],
                        swsel[:, r0_ + ti * EE:r0_ + (ti + 1) * EE],
                        ssum2[:, par * NTB + ti:par * NTB + ti + 1])
                P.D("vector")
                P.E("vector", "engine_nop", inc=(sV, 1))
                return {"cx": cx, "v_ln2f": v_ln2f, "v_we": sV.bump(),
                        "par": par}

            nchb = NCH_B if LEVEL >= 9 else 0
            wq = {}
            if nchb > 0:
                prep[0] = prepB(0)
                aft0 = [(sT, sT.n),
                        (sSX[0], a_stores[0]), (sSX[1], a_stores[1])]
                for m in range(3):
                    wq[m] = wload(m % 9, m % 3,
                                  after=(aft0 if m == 0 else [(sT, sT.n)]))
            for ch in range(nchb):
                pp = prep[ch]
                cx = pp["cx"]
                par = pp["par"]
                v_ln2f = pp["v_ln2f"]
                v_we = pp["v_we"]
                f0 = par * 8 * CHB
                r0_ = par * NTB * EE

                ev = None
                vhid = {}
                t_hid_done = {}
                for stg in range(EE + 2):
                    if stg <= EE:
                        e = stg
                        m_ = ch * 9 + e
                        slot = m_ % 3
                        wv_e = wq[m_][0]
                        hid_last = None
                        for hc in range(8):
                            mms = [(sw1[slot][:, dc * HFF + hc * 128:
                                              dc * HFF + (hc + 1) * 128],
                                    sln2f[:, f0 + dc * CHB:
                                          f0 + dc * CHB + 512])
                                   for dc in range(8)]

                            def mk_hid_evac(hc=hc, e=e):
                                def evf(bank, tv):
                                    pr_ = hc % 2
                                    P.E("scalar", "activation",
                                        srelu[:, pr_ * 512:(pr_ + 1) * 512],
                                        bank[:, 0:512], ACTF.Relu,
                                        bias=0.0, scale=1.0,
                                        waits=[(sT, tv)], inc=(sS, 1))
                                    sval = sS.bump()
                                    P.E("vector", "tensor_tensor",
                                        shid[e % 2][:, hc * 512:
                                                    (hc + 1) * 512],
                                        srelu[:, pr_ * 512:(pr_ + 1) * 512],
                                        srelu[:, pr_ * 512:(pr_ + 1) * 512],
                                        ALU.mult,
                                        waits=[(sS, sval)], inc=(sV, 1))
                                    return (sV, sV.bump())
                                return evf

                            w0 = ([wv_e, (sV, v_ln2f)] if hc == 0 else [])
                            hid_last = proj_groupB((pA, pB), mms,
                                                   mk_hid_evac(), extra=w0)
                        vhid[e] = hid_last[1]
                        t_hid_done[m_] = sT.n
                    if stg == 6 and ch + 1 < NCH_B:
                        prep[ch + 1] = prepB(ch + 1)
                    if stg >= 1:
                        e = stg - 1
                        m_ = ch * 9 + e
                        slot = m_ % 3
                        for ti in range(NTB):
                            for jh in range(2):
                                mms = [(shid[e % 2][:, hc * 512 + ti * 128:
                                                    hc * 512 +
                                                    (ti + 1) * 128],
                                        sw2[slot][:, hc * D + jh * 512:
                                                  hc * D + (jh + 1) * 512])
                                       for hc in range(8)]

                                def mk_acc_evac(e=e, ti=ti, jh=jh):
                                    def evf(bank, tv):
                                        c0 = ti * D + jh * 512
                                        if e == 0:
                                            w_acc = [(sT, tv), (sV, v_we)]
                                            if (ti == 0 and jh == 0
                                                    and ch >= 1):
                                                w_acc.append(
                                                    (sST,
                                                     out_stored[ch - 1]))
                                            P.E("vector",
                                                "tensor_scalar_mul",
                                                sacc[:, c0:c0 + 512],
                                                bank[:, 0:512],
                                                swe[:, r0_ + ti * EE:
                                                    r0_ + ti * EE + 1],
                                                waits=w_acc,
                                                inc=(sV, 1))
                                        elif e < EE:
                                            P.E("vector",
                                                "scalar_tensor_tensor",
                                                sacc[:, c0:c0 + 512],
                                                bank[:, 0:512],
                                                swe[:, r0_ + ti * EE + e:
                                                    r0_ + ti * EE + e + 1],
                                                sacc[:, c0:c0 + 512],
                                                ALU.mult, ALU.add,
                                                waits=[(sT, tv)],
                                                inc=(sV, 1))
                                        else:
                                            P.E("vector", "tensor_tensor",
                                                sacc[:, c0:c0 + 512],
                                                bank[:, 0:512],
                                                sacc[:, c0:c0 + 512],
                                                ALU.add,
                                                waits=[(sT, tv)])
                                            P.D("vector")
                                            P.E("vector", "tensor_tensor",
                                                sacc[:, c0:c0 + 512],
                                                sacc[:, c0:c0 + 512],
                                                cx[:, c0:c0 + 512],
                                                ALU.add, inc=(sV, 1))
                                        return (sV, sV.bump())
                                    return evf

                                w0 = ([(sV, vhid[e]), wq[m_][1]]
                                      if (ti == 0 and jh == 0) else [])
                                ev = proj_groupB((pC, pD), mms,
                                                 mk_acc_evac(), extra=w0)
                        mp = m_ + 3
                        if mp < nchb * 9:
                            wq[mp] = wload(mp % 9, mp % 3,
                                           after=[(sT, t_hid_done[m_])],
                                           after2=(sT, sT.n))
                v_outc = ev[1]
                xfree[ch] = v_outc

                P.W("sync", sV, v_outc)
                dstO = out_d[ch * CHB:(ch + 1) * CHB, :]
                P.E("sync", "dma_start",
                    out=dstO.rearrange("(t p) d -> p t d", p=128),
                    in_=sacc[:].rearrange("p (t d) -> p t d", d=D),
                    inc=(sST, 16))
                out_stored[ch] = sST.bump(16)

            if sST.n:
                P.W("sync", sST, sST.n)
            P.flush()

    return nc



_S = 2048
_NSEQ = 2


def kernel(**inputs):
    consts = host_constants(_S)
    in_maps = []
    for core in range(NCORES):
        m = host_prep(inputs, core, _S, _NSEQ)
        m.update(consts)
        in_maps.append(m)
    nc = build(S=_S, NSEQ=_NSEQ, CHA=512, CHB=512)
    res = run_bass_kernel_spmd(nc, in_maps, core_ids=list(range(NCORES)))
    outs = [np.asarray(r["out"]) for r in res.results]
    return np.concatenate(outs, axis=0).reshape(16, _S, D)


if __name__ == "__main__":
    nc = build()
    n = nc.m.functions[0]
    print("blocks:", len(n.blocks))
    print("instructions:", sum(len(b.instructions) for b in n.blocks))
